# version 10 (final)
# baseline (speedup 1.0000x reference)
import numpy as np

import concourse.bass as bass
import concourse.mybir as mybir
from concourse.alu_op_type import AluOpType
from concourse.bass_utils import run_bass_kernel_spmd

F32 = mybir.dt.float32
BF16 = mybir.dt.bfloat16
U8 = mybir.dt.uint8
AF = mybir.ActivationFunctionType
OP = AluOpType
AXX = mybir.AxisListType.X

T = 2048
C = 1024
NCH = C // 128
NT = T // 128
NBI = 4
NCORE = 8

LN2 = float(np.log(np.float64(2.0)))
IL2 = float(1.0 / np.log(np.float64(2.0)))
C23 = float(1.5 * 2.0 ** 23)
C52 = float(1.5 * 2.0 ** 52)


class PB:
    def __init__(self):
        self.nc = bass.Bass("TRN2", target_bir_lowering=False)
        self._stack = []
        self.cnt = {}
        self._lastwait = {}
        self._dirty = {e: [] for e in ("pe", "ve", "act", "gp", "sy")}
        self._rset = {e: [] for e in ("pe", "ve", "act", "gp", "sy")}
        self.guards = {}

    @property
    def TE(self): return self.nc.tensor
    @property
    def V(self): return self.nc.vector
    @property
    def A(self): return self.nc.scalar
    @property
    def G(self): return self.nc.gpsimd
    @property
    def SY(self): return self.nc.sync

    def engname(self, eng):
        return {id(self.nc.tensor): "pe", id(self.nc.vector): "ve",
                id(self.nc.scalar): "act", id(self.nc.gpsimd): "gp",
                id(self.nc.sync): "sy"}[id(eng)]

    def sem(self, name):
        h = self._enter(self.nc.semaphore(name))
        self.cnt[name] = 0
        setattr(self, "sem_" + name, h)

    def dsem(self, key):
        name = "d_" + key
        if name not in self.cnt:
            self.sem(name)
        return name

    def inc(self, instr, semname, k=1):
        instr.then_inc(getattr(self, "sem_" + semname), k)
        self.cnt[semname] += k
        return (semname, self.cnt[semname])

    def ck(self, semname):
        return (semname, self.cnt[semname])

    def wait(self, eng, ckpt):
        if ckpt is None:
            return
        semname, val = ckpt
        if val <= 0:
            return
        key = (self.engname(eng), semname)
        if self._lastwait.get(key, -1) >= val:
            return
        eng.wait_ge(getattr(self, "sem_" + semname), val)
        self._lastwait[key] = val

    def guard(self, bufname, eng):
        self.wait(eng, self.guards.get(bufname))

    def setguard(self, bufname, ck):
        self.guards[bufname] = ck

    _DTSZ = {}

    @classmethod
    def _iv(cls, ap):
        try:
            name = ap.tensor.name
        except Exception:
            return None
        try:
            sz = cls._DTSZ[ap.dtype]
        except KeyError:
            sz = np.dtype(mybir.dt.np(ap.dtype)).itemsize
            cls._DTSZ[ap.dtype] = sz
        lo = ap.offset
        span = 1
        for step, cnt2 in list(ap.ap)[1:]:
            span += abs(step) * (cnt2 - 1)
        return (name, lo * sz, (lo + span) * sz)

    @staticmethod
    def _ovl(ivs, lst):
        return any(n == dn and lo < dhi and hi > dlo
                   for (n, lo, hi) in ivs for (dn, dlo, dhi) in lst)

    def em(self, eng, reads=(), writes=()):
        en = self.engname(eng)
        dirty = self._dirty[en]
        rset = self._rset[en]

        def iva(aps):
            o = []
            for a in aps:
                if a is None or isinstance(a, (int, float)):
                    continue
                iv = self._iv(a)
                if iv:
                    o.append(iv)
            return o
        riv = iva(reads)
        wiv = iva(writes)
        if self._ovl(riv, dirty) or self._ovl(wiv, dirty) or self._ovl(wiv, rset):
            eng.drain()
            dirty.clear()
            rset.clear()
        dirty.extend(wiv)
        rset.extend(riv)
        return eng

    def drainall(self, eng):
        en = self.engname(eng)
        if self._dirty[en] or self._rset[en]:
            eng.drain()
            self._dirty[en].clear()
            self._rset[en].clear()

    def mark(self, eng):
        name = self.engname(eng)
        name = {"pe": "pe", "ve": "ve", "act": "act", "gp": "gp"}[name]
        self.drainall(eng)
        return self.inc(eng.nop(), name, 1)

    def _s(self, x):
        return x if hasattr(x, "tensor") else None

    def ts(self, eng, out, in0, s1, s2, op0, op1=None, accum_out=None):
        self.em(eng, [in0, self._s(s1), self._s(s2)], [out, accum_out])
        if op1 is None:
            return eng.tensor_scalar(out, in0, s1, None, op0)
        return eng.tensor_scalar(out, in0, s1, s2, op0, op1, accum_out=accum_out)

    def tt(self, eng, out, in0, in1, op):
        self.em(eng, [in0, in1], [out])
        return eng.tensor_tensor(out, in0, in1, op)

    def stt(self, eng, out, in0, scalar, in1, op0, op1):
        self.em(eng, [in0, in1, self._s(scalar)], [out])
        return eng.scalar_tensor_tensor(out, in0, scalar, in1, op0, op1)

    def act(self, out, in_, func, bias=0.0, scale=1.0, accum_out=None):
        self.em(self.A, [in_, self._s(bias), self._s(scale)], [out, accum_out])
        return self.A.activation(out, in_, func, bias=bias, scale=scale,
                                 accum_out=accum_out)

    def copy(self, eng, out, in_):
        self.em(eng, [in_], [out])
        return eng.tensor_copy(out, in_)

    def recip(self, out, in_):
        self.em(self.V, [in_], [out])
        return self.V.reciprocal(out, in_)

    def red(self, eng, out, in_, op, negate=False):
        self.em(eng, [in_], [out])
        return eng.tensor_reduce(out, in_, axis=AXX, op=op, negate=negate)

    def scan(self, out, d0, d1, init, op0, op1):
        self.em(self.V, [d0, d1], [out])
        return self.V.tensor_tensor_scan(out, d0, d1, init, op0, op1)

    def mm(self, out, lhsT, rhs, start=True, stop=True, tile_position=None):
        self.em(self.TE, [lhsT, rhs], [out])
        return self.TE.matmul(out, lhsT, rhs, start=start, stop=stop,
                              tile_position=tile_position)

    def tr(self, out, in_, iden):
        self.em(self.TE, [in_, iden], [out])
        return self.TE.transpose(out, in_, iden)

    def _enter(self, cm):
        v = cm.__enter__()
        self._stack.append(cm)
        return v

    def sb(self, name, shape, dtype):
        t = self._enter(self.nc.sbuf_tensor(name, list(shape), dtype))
        setattr(self, name, t)
        return t

    def psum(self, name, shape, dtype):
        t = self._enter(self.nc.psum_tensor(name, list(shape), dtype))
        setattr(self, name, t)
        return t

    def close(self):
        while self._stack:
            self._stack.pop().__exit__(None, None, None)


def build_program(nbi=NBI, layers=(0, 1, 2, 3), taps=()):
    b = PB()
    nc = b.nc
    taps = set(taps)
    b.tapcols = {}

    x_in = nc.dram_tensor("x", [nbi, T, C], F32, kind="ExternalInput").ap()
    out = nc.dram_tensor("out", [nbi, T, C], F32, kind="ExternalOutput").ap()

    W = {}

    def wdecl(name, shape, dtype=F32):
        W[name] = nc.dram_tensor("w_" + name, list(shape), dtype,
                                 kind="ExternalInput").ap()

    wdecl("iden", [128, 128])
    wdecl("onesrow", [1, 128])
    wdecl("onescol", [128, 1])
    wdecl("ltmask", [128, 128])
    wdecl("causal_add", [128, 128])
    if 0 in layers:
        wdecl("l0_wq_swz", [NCH, 128, NCH, 128])
        wdecl("l0_wkv_t", [C, 2 * C])
        wdecl("l0_wproj_swz", [NCH, 128, NCH, 128])
        wdecl("l0_bproj", [128, NCH])
        wdecl("l0_w1_t", [C, 4 * C])
        wdecl("l0_b1", [128, 32])
        wdecl("l0_w2_t", [4 * C, C])
        wdecl("l0_b2", [128, NCH])
        wdecl("l0_dropmask", [nbi, NCH, 128, T], U8)
    if 1 in layers:
        for nm in ("tmx", "tmw", "tmk", "tmv", "tmr", "tmg", "tdec", "ucol",
                   "lng", "lnb"):
            wdecl("l1_" + nm, [128, NCH])
        wdecl("l1_maaw1_swz", [3, 128, NCH, 64])
        wdecl("l1_maaw2a", [64, C])
        wdecl("l1_maaw2b", [64, C])
        wdecl("l1_maaw2c", [32, C])
        wdecl("l1_tdw1_swz", [1, 128, NCH, 64])
        wdecl("l1_tdw2", [64, C])
        for nm in ("wr", "wk", "wv", "wg", "wo"):
            wdecl("l1_" + nm + "_swz", [NCH, 128, NCH, 128])
        wdecl("l1f_w1_swz", [32, 128, NCH, 128])
        wdecl("l1f_b1", [128, 32])
        wdecl("l1f_w2_swz", [NCH, 128, 32, 128])
        wdecl("l1f_b2", [128, NCH])
    if 2 in layers:
        wdecl("l2_wq_swz", [NCH, 128, NCH, 128])
        wdecl("l2_wk_swz", [NCH, 128, NCH, 128])
        wdecl("l2_wv_swz", [NCH, 128, NCH, 128])
        for nm in ("bq", "bk", "bv"):
            wdecl("l2_" + nm, [128, NCH])
        wdecl("l2f_w1_swz", [32, 128, NCH, 128])
        wdecl("l2f_w2_swz", [32, 128, NCH, 128])
        wdecl("l2f_h1_swz", [32, 128, 32, 128])
        wdecl("l2f_h2_swz", [32, 128, 32, 128])
        wdecl("l2f_w3_swz", [NCH, 128, 32, 128])
        wdecl("l2f_s1", [128, 32])
        wdecl("l2f_s2", [128, 32])
        wdecl("l2f_sh1", [128, 32])
        wdecl("l2f_sh2", [128, 32])
        wdecl("l2f_s3", [128, NCH])
    if 3 in layers:
        wdecl("l3_lnw", [128, NCH])
        wdecl("l3_lnb", [128, NCH])
        wdecl("l3_inq_swz", [NCH, 128, NCH, 128])
        wdecl("l3_ink_swz", [NCH, 128, NCH, 128])
        wdecl("l3_inv_swz", [NCH, 128, NCH, 128])
        wdecl("l3_inprojb", [128, 3 * NCH])
        wdecl("l3_outproj_swz", [NCH, 128, NCH, 128])
        wdecl("l3_outb", [128, NCH])
        wdecl("l3_w0t", [4, 2, 128, 256])
        wdecl("l3_w1", [4, 2, 128, 256])
        wdecl("l3_w2t", [4, 2, 128, 256])
        wdecl("l3_w1t", [4, 2, 128, 256])
        wdecl("l3f_wg", [C, 8])
        wdecl("l3f_bgrow", [128, 8])
        wdecl("l3f_we", [C, C])
        wdecl("l3f_bebc", [128, C])

    w1q_d = nc.dram_tensor("w1q_d", [NCH, 128, 4 * C], BF16, kind="Internal").ap()
    l1s_d = nc.dram_tensor("l1s_d", [5, NCH, 128, T], F32, kind="Internal").ap()
    l3x2_d = nc.dram_tensor("l3x2_d", [NCH, 128, T], F32, kind="Internal").ap()
    w2q_d = nc.dram_tensor("w2q_d", [32, 128, C], BF16, kind="Internal").ap()

    b.sb("XR", [128, NCH, T], F32)
    b.sb("XN", [128, NCH, T], BF16)
    b.sb("BIGA", [128, 8 * T], BF16)
    b.sb("BIGB", [128, 8 * T], BF16)
    b.sb("WS", [128, 4616], BF16)
    b.sb("SCF", [128, 2272], F32)
    b.sb("CST", [128, 258], BF16)
    b.sb("IDENF", [128, 128], F32)
    b.sb("CAUS", [128, 128], F32)
    b.sb("ONESR", [1, 128], F32)
    b.sb("ONECF", [128, 1], F32)
    b.sb("BIAS", [128, 384], F32)
    b.sb("COLS", [128, 64], F32)
    b.sb("CONSTC", [128, 8], F32)
    b.sb("KVACC", [128, 520], F32)
    b.sb("ROW1", [1, T], F32)
    b.sb("ROW2", [1, T], F32)
    b.sb("ROW0T", [1, T], F32)
    BAF = b.BIGA[:, :].bitcast(F32)
    BBF = b.BIGB[:, :].bitcast(F32)
    XNflat = b.XN[:, :, :].rearrange("p c t -> p (c t)")
    XNF = XNflat.bitcast(F32)
    BA = b.BIGA[:, :].rearrange("p (c n) -> p c n", n=T)
    BB = b.BIGB[:, :].rearrange("p (c n) -> p c n", n=T)

    b.psum("PSB", [128, 2048], F32)
    b.psum("PS4", [128, 512], F32)
    b.psum("PS5", [128, 512], F32)
    b.psum("PS6", [128, 512], F32)
    b.psum("PT", [128, 1024], BF16)

    for s in ("din", "gdma", "pe", "ve", "act", "gp", "out", "spill"):
        b.sem(s)

    TE, V, A, G, SY = b.TE, b.V, b.A, b.G, b.SY
    IDB = b.CST[:, 0:128]
    LTM = b.CST[:, 128:256]
    ONEC = b.CST[:, 256:257]

    b.inc(G.dma_start(IDB, W["iden"]), b.dsem("kconst"), 16)
    b.inc(G.dma_start(LTM, W["ltmask"]), b.dsem("kconst"), 16)
    b.inc(G.dma_start(b.IDENF[:, :], W["iden"]), b.dsem("kconst"), 16)
    b.inc(G.dma_start(b.CAUS[:, :], W["causal_add"]), b.dsem("kconst"), 16)
    b.inc(G.dma_start(b.ONESR[:, :], W["onesrow"]), b.dsem("kconst"), 16)
    b.inc(G.dma_start(ONEC, W["onescol"]), b.dsem("kconst"), 16)
    b.inc(G.dma_start(b.ONECF[:, :], W["onescol"]), b.dsem("kconst"), 16)
    CONSTS = b.ck("d_kconst")

    _ccol = {}

    def ccol(v):
        v = float(v)
        if v not in _ccol:
            slot = len(_ccol)
            assert slot < 7
            ap = b.CONSTC[:, slot:slot + 1]
            b.em(G, [], [ap])
            G.memset(ap, v)
            ck = b.mark(G)
            _ccol[v] = (ap, ck)
        ap, ck = _ccol[v]
        b.wait(A, ck)
        b.wait(V, ck)
        return ap

    ZCOL = b.CONSTC[:, 7:8]
    b.em(G, [], [ZCOL])
    G.memset(ZCOL, 1.0)
    zck = b.mark(G)
    b.wait(V, zck)
    b.ts(V, ZCOL, ZCOL, 2.0 ** -30, -1.0, OP.add, OP.add)
    b.ts(V, ZCOL, ZCOL, 0.0, None, OP.is_equal)
    ZRDY = b.mark(V)
    b.wait(G, ZRDY)
    b.wait(A, ZRDY)

    def rne(OUT, VIN, T, pre_mul=1.0, pre_add=0.0):
        b.ts(V, T, VIN, pre_mul, pre_add + C52, OP.mult, OP.add)
        b.ts(V, T, T, -C52, None, OP.add)
        b.ts(V, OUT, VIN, pre_mul, pre_add + C23, OP.mult, OP.add)
        b.ts(V, OUT, OUT, -C23, None, OP.add)
        b.stt(V, OUT, OUT, ZCOL, T, OP.mult, OP.add)

    def rne2(OUT, VIN, pre_mul=1.0, pre_add=0.0):
        assert pre_add == 0.0
        b.ts(V, OUT, VIN, pre_mul, pre_add + C52, OP.mult, OP.add)
        b.ts(V, OUT, OUT, -C52, None, OP.add)
        b.ts(V, VIN, VIN, pre_mul, pre_add + C23, OP.mult, OP.add)
        b.ts(V, VIN, VIN, -C23, None, OP.add)
        b.stt(V, OUT, VIN, ZCOL, OUT, OP.mult, OP.add)

    BOFF = {}

    def load_bias(name, width):
        BOFF[name] = load_bias.off
        b.inc(SY.dma_start(b.BIAS[:, load_bias.off:load_bias.off + width],
                           W[name]), b.dsem("kbias"), 16)
        load_bias.off += width
    load_bias.off = 0

    def balloc(name, width):
        BOFF[name] = load_bias.off
        load_bias.off += width

    if 0 in layers:
        load_bias("l0_bproj", NCH)
        load_bias("l0_b1", 32)
        load_bias("l0_b2", NCH)
        balloc("l0_b1q", 32)
        balloc("l0_b2q", NCH)
    if 1 in layers:
        for nm in ("l1_tmx", "l1_tmw", "l1_tmk", "l1_tmv", "l1_tmr", "l1_tmg",
                   "l1_tdec", "l1_ucol", "l1_lng", "l1_lnb"):
            load_bias(nm, NCH)
        load_bias("l1f_b1", 32)
        load_bias("l1f_b2", NCH)
    if 2 in layers:
        for nm, wdt in (("l2_bq", NCH), ("l2_bk", NCH), ("l2_bv", NCH),
                        ("l2f_s3", NCH)):
            load_bias(nm, wdt)
    if 3 in layers:
        for nm, wdt in (("l3_lnw", NCH), ("l3_lnb", NCH),
                        ("l3_inprojb", 3 * NCH), ("l3_outb", NCH),
                        ("l3f_bgrow", 8)):
            load_bias(nm, wdt)
    BIASCK = b.ck("d_kbias")

    def bcol(name, j=0):
        return b.BIAS[:, BOFF[name] + j:BOFF[name] + j + 1]

    def brange(name, w):
        return b.BIAS[:, BOFF[name]:BOFF[name] + w]

    def tap(name, srcs):
        if name not in taps:
            return
        tot = sum(cc for _, cc in srcs)
        np_ = srcs[0][0].shape[0]
        o = nc.dram_tensor("tap_" + name, [np_, tot], F32,
                           kind="ExternalOutput").ap()
        b.tapcols[name] = tot
        phase_sync((A, G))
        off = 0
        for src, cc in srcs:
            b.inc(G.dma_start(o[:, off:off + cc], src), b.dsem("ktap"), 16)
            off += cc
        ock = b.ck("d_ktap")
        for eng in (TE, V, A, G, SY):
            b.wait(eng, ock)

    def phase_sync(engines=None):
        if engines is None:
            engines = (TE, V, A, G, SY)
        cks = [b.ck(s) for s in ("pe", "ve", "act", "gp", "spill")]
        cks += [b.ck(s) for s in list(b.cnt) if s.startswith("d_")]
        for eng in engines:
            b.drainall(eng)
            for ckk in cks:
                b.wait(eng, ckk)

    def ld(dst, src, key):
        return b.inc(G.dma_start(dst, src), b.dsem(key), 16)

    def ldh(dst, src, key, eng=None):
        return b.inc((eng or SY).dma_start(dst, src), b.dsem(key), 16)

    def q_chain(H, T1, T2, OUTF, mb, mine, relu_first=False,
                exact_ties=False, T3=None, T4B=None):
        if relu_first:
            b.ts(V, H, H, 0.0, None, OP.max)
        v0 = b.mark(V)
        b.wait(A, v0)
        b.act(T1, H, AF.Abs)
        ack = b.mark(A)
        b.wait(V, ack)
        b.ts(V, T1, T1, 2.0 ** mine, None, OP.max)
        v1 = b.mark(V)
        b.wait(A, v1)
        b.act(T1, T1, AF.Ln)
        ack = b.mark(A)
        b.wait(V, ack)
        b.ts(V, T2, T1, IL2, -0.5, OP.mult, OP.add)
        rne2(T1, T2)
        vck = b.mark(V)
        b.wait(A, vck)
        b.act(T2, T1, AF.Exp, scale=-LN2, bias=ccol(float(mb) * LN2))
        ack = b.mark(A)
        b.wait(V, ack)
        b.tt(V, T2, H, T2, OP.mult)
        if not exact_ties:
            b.ts(V, H, T2, 1.0, C52, OP.mult, OP.add)
            b.ts(V, H, H, -C52, None, OP.add)
            b.ts(V, T2, T2, 1.0, C23, OP.mult, OP.add)
            b.ts(V, T2, T2, -C23, None, OP.add)
            b.stt(V, H, T2, ZCOL, H, OP.mult, OP.add)
            vck2 = b.mark(V)
            b.wait(A, vck2)
            b.act(T1, T1, AF.Exp, scale=LN2, bias=ccol(-float(mb) * LN2))
            ack = b.mark(A)
            b.wait(V, ack)
            b.tt(V, OUTF, H, T1, OP.mult)
        else:
            rne(T3, T2, H)
            b.tt(V, T4B, T2, T3, OP.subtract)
            b.ts(V, H, T4B, 0.4999, None, OP.is_ge)
            b.ts(V, T4B, T4B, -0.4999, None, OP.is_le)
            b.tt(V, T4B, T4B, H, OP.add)
            rne2(H, T2, 0.5, 0.0)
            b.stt(V, T2, H, 2.0, T3, OP.mult, OP.subtract)
            b.tt(V, T2, T4B, T2, OP.mult)
            b.tt(V, T3, T3, T2, OP.add)
            vck2 = b.mark(V)
            b.wait(A, vck2)
            b.act(T1, T1, AF.Exp, scale=LN2, bias=ccol(-float(mb) * LN2))
            ack = b.mark(A)
            b.wait(V, ack)
            b.tt(V, OUTF, T3, T1, OP.mult)
        return b.mark(V)

    def f2f8(OUT, X, T1, T2):
        b.ts(V, T1, X, 256.0, C52, OP.mult, OP.add)
        b.ts(V, T1, T1, -C52, None, OP.add)
        b.ts(V, T2, X, 256.0, C23, OP.mult, OP.add)
        b.ts(V, T2, T2, -C23, None, OP.add)
        b.stt(V, T1, T2, ZCOL, T1, OP.mult, OP.add)
        b.ts(V, OUT, T1, 2.0 ** -8, None, OP.mult)
        return b.mark(V)

    def emit_layernorm(precise=False):
        ROW0 = b.ROW0T[:, :]
        phase_sync()
        for c in range(NCH):
            b.copy(V, b.XN[:, c, :], b.XR[:, c, :])
        vck = b.mark(V)
        b.wait(TE, vck)
        b.wait(TE, CONSTS)
        for c in range(NCH):
            for nb2 in range(T // 512):
                sl = slice(nb2 * 512, nb2 * 512 + 512)
                if precise:
                    b.mm(b.PSB[0:1, sl], b.ONECF[:, :], b.XR[:, c, sl],
                         start=(c == 0), stop=(c == NCH - 1))
                else:
                    b.mm(b.PSB[0:1, sl], ONEC, b.XN[:, c, sl],
                         start=(c == 0), stop=(c == NCH - 1))
        pe1 = b.mark(TE)
        b.wait(V, pe1)
        b.ts(V, ROW0, b.PSB[0:1, 0:T], 1.0 / C, None, OP.mult)
        vs1 = b.mark(V)
        b.wait(TE, vs1)
        b.wait(A, vck)
        sq = b.WS[:, 0:T]
        sqf = b.SCF[:, 224:2272]
        for c in range(NCH):
            b.guard("lnsq", A)
            if precise:
                b.act(sqf, b.XR[:, c, :], AF.Square)
            else:
                b.act(sq, b.XN[:, c, :], AF.Square)
            ack = b.mark(A)
            b.wait(TE, ack)
            for nb2 in range(T // 512):
                sl = slice(nb2 * 512, nb2 * 512 + 512)
                if precise:
                    b.mm(b.PSB[0:1, sl], b.ONECF[:, :], sqf[:, sl],
                         start=(c == 0), stop=(c == NCH - 1))
                else:
                    b.mm(b.PSB[0:1, sl], ONEC, sq[:, sl],
                         start=(c == 0), stop=(c == NCH - 1))
            b.setguard("lnsq", b.mark(TE))
        pe2 = b.ck("pe")
        b.wait(V, pe2)
        b.ts(V, b.ROW1[:, :], b.PSB[0:1, 0:T], 1.0 / C, None, OP.mult)
        b.tt(V, b.ROW2[:, :], ROW0, ROW0, OP.mult)
        b.tt(V, b.ROW1[:, :], b.ROW1[:, :], b.ROW2[:, :], OP.subtract)
        b.ts(V, b.ROW1[:, :], b.ROW1[:, :], 1e-5, None, OP.add)
        VEP = b.PSB[0:1, 0:T]
        b.copy(V, VEP, b.ROW1[:, :])
        b.recip(b.ROW2[:, :], b.ROW1[:, :])
        v2 = b.mark(V)
        b.wait(A, v2)
        b.act(b.ROW1[:, :], b.ROW2[:, :], AF.Sqrt)
        a2 = b.mark(A)
        b.wait(V, a2)
        b.tt(V, b.ROW2[:, :], b.ROW1[:, :], b.ROW1[:, :], OP.mult)
        b.tt(V, b.ROW2[:, :], b.ROW2[:, :], VEP, OP.mult)
        b.ts(V, b.ROW2[:, :], b.ROW2[:, :], -0.5, 1.5, OP.mult, OP.add)
        b.tt(V, b.ROW1[:, :], b.ROW1[:, :], b.ROW2[:, :], OP.mult)
        b.tt(V, b.ROW2[:, :], ROW0, b.ROW1[:, :], OP.mult)
        b.ts(V, b.ROW2[:, :], b.ROW2[:, :], -1.0, None, OP.mult)
        v3 = b.mark(V)
        b.wait(TE, v3)
        for nb2 in range(T // 512):
            sl = slice(nb2 * 512, nb2 * 512 + 512)
            b.guard("ps4", TE)
            b.guard("ps5", TE)
            b.mm(b.PS4[:, :], b.ONESR[:, :], b.ROW1[:, sl])
            b.mm(b.PS5[:, :], b.ONESR[:, :], b.ROW2[:, sl])
            pb = b.mark(TE)
            b.wait(V, pb)
            for c in range(NCH):
                b.tt(V, b.SCF[:, 0:512], b.XR[:, c, sl], b.PS4[:, :], OP.mult)
                b.tt(V, b.XN[:, c, sl], b.SCF[:, 0:512], b.PS5[:, :], OP.add)
            vm = b.mark(V)
            b.setguard("ps4", vm)
            b.setguard("ps5", vm)
        return b.ck("ve")

    def ln_apply_block(dst3, tsl, width):
        b.guard("ps4", TE)
        b.guard("ps5", TE)
        b.mm(b.PS4[:, 0:width], b.ONESR[:, :], b.ROW1[:, tsl])
        b.mm(b.PS5[:, 0:width], b.ONESR[:, :], b.ROW2[:, tsl])
        pb = b.mark(TE)
        b.wait(V, pb)
        for c in range(NCH):
            b.tt(V, dst3[:, c, :], b.XR[:, c, tsl], b.PS4[:, 0:width], OP.mult)
            b.tt(V, dst3[:, c, :], dst3[:, c, :], b.PS5[:, 0:width], OP.add)
        vm = b.mark(V)
        b.setguard("ps4", vm)
        b.setguard("ps5", vm)
        return vm

    def proj_wstat(wap, src, noc, evac, src_nch=NCH, nblk=512, wsoff=0,
                   bf16_slab=False):
        slabw = src_nch * 128
        slab_g = [b.ck("pe"), b.ck("pe")]
        for oc in range(noc):
            slab = b.WS[:, wsoff + (oc % 2) * slabw:
                        wsoff + (oc % 2) * slabw + slabw]
            slab3 = slab.rearrange("p (c e) -> p c e", e=128)
            b.wait(G, slab_g[oc % 2])
            lck = ld(slab3, wap[oc], "ws%d" % (oc % 2))
            b.wait(TE, lck)
            for nb in range(T // nblk):
                tsl = slice(nb * nblk, nb * nblk + nblk)
                pt = (b.PS4, b.PS5)[nb % 2]
                pname = ("ps4", "ps5")[nb % 2]
                b.guard(pname, TE)
                for cch in range(src_nch):
                    b.mm(pt[:, 0:nblk], slab3[:, cch, :], src[:, cch, tsl],
                         start=(cch == 0), stop=(cch == src_nch - 1))
                pe_ck = b.mark(TE)
                cck = evac(oc, nb, pt[:, 0:nblk], tsl, pe_ck)
                b.setguard(pname, cck)
            slab_g[oc % 2] = b.ck("pe")

    def emit_l0_attn(bi, ln_ck):
        phase_sync()
        for c in range(NCH):
            ld(BA[:, c, :], W["l0_wkv_t"][128 * c:128 * c + 128, :], "biga")
        wkv = b.ck("d_biga")
        vstage = b.BIGB[:, 0:16 * 65]
        for h in range(16):
            b.em(G, [], [vstage[:, h * 65 + 64:h * 65 + 65]])
            G.memset(vstage[:, h * 65 + 64:h * 65 + 65], 1.0)
        gset = b.mark(G)
        kf = b.BIGB[:, 16 * 65:16 * 65 + 1024]
        b.wait(TE, wkv)
        b.wait(TE, ln_ck)
        b.wait(TE, gset)
        for tt in range(NT):
            tsl = slice(tt * 128, tt * 128 + 128)
            b.guard("psb", TE)
            for nb in range(4):
                esl = slice(nb * 512, nb * 512 + 512)
                for c in range(NCH):
                    b.mm(b.PSB[:, esl], b.XN[:, c, tsl], BA[:, c, esl],
                         start=(c == 0), stop=(c == NCH - 1))
            pec = b.mark(TE)
            b.wait(V, pec)
            b.ts(V, b.SCF[:, 0:1024], b.PSB[:, 0:1024], 0.0, None, OP.min)
            b.ts(V, b.SCF[:, 1024:2048], b.PSB[:, 0:1024], 0.0, None, OP.max)
            vp = b.mark(V)
            b.wait(A, vp)
            b.act(b.SCF[:, 0:1024], b.SCF[:, 0:1024], AF.Exp)
            ap = b.mark(A)
            b.wait(V, ap)
            b.tt(V, kf, b.SCF[:, 0:1024], b.SCF[:, 1024:2048], OP.add)
            vsv = vstage.rearrange("p (h e) -> p h e", e=65)[:, :, 0:64]
            vv = b.PSB[:, :].rearrange("p (h d) -> p h d", d=64)[:, 16:32, :]
            b.copy(V, vsv, vv)
            vkv = b.mark(V)
            b.setguard("psb", vkv)
            b.wait(TE, vkv)
            b.guard("ps4", TE)
            b.guard("ps5", TE)
            kfh = kf.rearrange("p (h d) -> p h d", d=64)
            for h in range(16):
                pt2 = b.PS4 if h < 8 else b.PS5
                pr = (h % 2) * 64
                pc = (h // 2) % 4
                b.mm(pt2[pr:pr + 64, pc * 65:pc * 65 + 65],
                     kfh[:, h, :], vstage[:, h * 65:h * 65 + 65],
                     tile_position=(0, pr))
            pkv = b.mark(TE)
            b.wait(V, pkv)
            if tt == 0:
                b.copy(V, b.KVACC[:, 0:260], b.PS4[:, 0:260])
                b.copy(V, b.KVACC[:, 260:520], b.PS5[:, 0:260])
            else:
                b.tt(V, b.KVACC[:, 0:260], b.KVACC[:, 0:260],
                     b.PS4[:, 0:260], OP.add)
                b.tt(V, b.KVACC[:, 260:520], b.KVACC[:, 260:520],
                     b.PS5[:, 0:260], OP.add)
            vka = b.mark(V)
            b.setguard("ps4", vka)
            b.setguard("ps5", vka)
        kvb = b.WS[:, 4096:4096 + 520]
        b.copy(V, kvb[:, 0:520], b.KVACC[:, 0:520])
        vkvb = b.mark(V)
        tap("l0_kv", [(kvb[:, 0:520], 520)])

        qf = b.BIGA[:, 0:2048]
        yT = BB
        biga_free = b.ck("pe")
        slab_g = [b.ck("pe"), b.ck("pe")]
        for qc in range(NCH):
            slab = b.WS[:, (qc % 2) * 1024:(qc % 2) * 1024 + 1024]
            slab3 = slab.rearrange("p (c e) -> p c e", e=128)
            b.wait(G, slab_g[qc % 2])
            lck = ld(slab3, W["l0_wq_swz"][qc], "ws%d" % (qc % 2))
            b.wait(TE, lck)
            b.guard("psb", TE)
            if qc == 0:
                b.wait(TE, biga_free)
            for nb in range(4):
                tsl = slice(nb * 512, nb * 512 + 512)
                for c in range(NCH):
                    b.mm(b.PSB[:, tsl], slab3[:, c, :], b.XN[:, c, tsl],
                         start=(c == 0), stop=(c == NCH - 1))
            peq = b.mark(TE)
            slab_g[qc % 2] = peq
            b.wait(V, peq)
            for half in range(2):
                hsl = slice(half * 1024, half * 1024 + 1024)
                b.ts(V, b.SCF[:, 0:1024], b.PSB[:, hsl], 0.0, None, OP.min)
                b.ts(V, b.SCF[:, 1024:2048], b.PSB[:, hsl], 0.0, None, OP.max)
                vp = b.mark(V)
                b.wait(A, vp)
                b.act(b.SCF[:, 0:1024], b.SCF[:, 0:1024], AF.Exp)
                ap2 = b.mark(A)
                b.wait(V, ap2)
                b.tt(V, qf[:, hsl], b.SCF[:, 0:1024], b.SCF[:, 1024:2048],
                     OP.add)
            vq = b.mark(V)
            b.setguard("psb", vq)
            b.wait(TE, vq)
            b.wait(TE, vkvb)
            for hh in range(2):
                h = 2 * qc + hh
                pr = (h % 2) * 64
                kvs = kvb[pr:pr + 64, (h // 2) * 65:(h // 2) * 65 + 65]
                b.guard("psb", TE)
                for tt in range(NT):
                    b.mm(b.PSB[:, tt * 128:tt * 128 + 65],
                         qf[pr:pr + 64, tt * 128:tt * 128 + 128], kvs,
                         tile_position=(pr, 0))
                pey = b.mark(TE)
                b.wait(V, pey)
                ps3 = b.PSB[:, :].rearrange("p (t e) -> p t e", e=128)
                b.ts(V, b.COLS[:, 0:16], ps3[:, :, 64:65], 1e-6, None, OP.add)
                b.drainall(V)
                b.recip(b.COLS[:, 16:32], b.COLS[:, 0:16])
                b.drainall(V)
                ystg = b.WS[:, 2048:3072]
                for tt in range(NT):
                    b.ts(V, ystg[:, tt * 64:tt * 64 + 64],
                         ps3[:, tt, 0:64], b.COLS[:, 16 + tt:17 + tt], None,
                         OP.mult)
                vy = b.mark(V)
                b.setguard("psb", vy)
                b.wait(TE, vy)
                for grp in range(2):
                    b.guard("pt", TE)
                    for j in range(8):
                        tt = grp * 8 + j
                        b.tr(b.PT[0:64, j * 128:j * 128 + 128],
                             ystg[:, tt * 64:tt * 64 + 64], IDB)
                    pck = b.mark(TE)
                    b.wait(V, pck)
                    src = b.PT[0:64, :].rearrange("p (j e) -> p j e", e=128)
                    dst = yT[pr:pr + 64, qc,
                             grp * 1024:grp * 1024 + 1024].rearrange(
                                 "p (j e) -> p j e", e=128)
                    b.copy(V, dst, src)
                    b.setguard("pt", b.mark(V))
        vyT = b.mark(V)
        tap("l0_yT", [(yT[:, c2, 0:256], 256) for c2 in range(NCH)])

        def evac(oc, nb, ps, tsl, pe_ck):
            b.wait(V, pe_ck)
            b.ts(V, b.SCF[:, 0:512], ps, bcol("l0_bproj", oc), None, OP.add)
            b.tt(V, b.XR[:, oc, tsl], b.XR[:, oc, tsl], b.SCF[:, 0:512],
                 OP.add)
            return b.mark(V)
        b.wait(TE, vyT)
        proj_wstat(W["l0_wproj_swz"], yT, NCH, evac)
        tap("l0_attn_out", [(b.XR[:, c2, 0:256], 256) for c2 in range(NCH)])

    def emit_l0_ffn_wprep():
        phase_sync()
        b.wait(V, BIASCK)
        b.copy(V, b.SCF[:, 0:32], brange("l0_b1", 32))
        b.mark(V)
        q_chain(b.SCF[:, 0:32], b.SCF[:, 32:64], b.SCF[:, 64:96],
                brange("l0_b1q", 32), 3, -6)
        b.copy(V, b.SCF[:, 0:8], brange("l0_b2", NCH))
        b.mark(V)
        q_chain(b.SCF[:, 0:8], b.SCF[:, 32:40], b.SCF[:, 64:72],
                brange("l0_b2q", NCH), 3, -6)
        for c in range(NCH):
            Hv = BAF[:, 0:4096]
            phase_sync((SY,))
            dck = ldh(Hv, W["l0_w1_t"][128 * c:128 * c + 128, :], "wprep")
            b.wait(V, dck)
            qck = q_chain(Hv, BAF[:, 4096:8192], BBF[:, 0:4096],
                          XNflat[:, 0:4096], 3, -6)
            b.wait(A, qck)
            b.inc(A.dma_start(w1q_d[c], XNflat[:, 0:4096]), "spill", 16)
            phase_sync((SY,))
        for g4 in range(8):
            Hv = BAF[:, 0:4096]
            phase_sync((SY,))
            dck = ldh(Hv.rearrange("p (j e) -> p j e", e=1024),
                      W["l0_w2_t"][512 * g4:512 * g4 + 512, :].rearrange(
                          "(j p) e -> p j e", p=128), "wprep")
            b.wait(V, dck)
            qck = q_chain(Hv, BAF[:, 4096:8192], BBF[:, 0:4096],
                          XNflat[:, 0:4096], 3, -6)
            b.wait(A, qck)
            b.inc(A.dma_start(
                w2q_d[4 * g4:4 * g4 + 4].rearrange("j p e -> p j e"),
                XNflat[:, 0:4096].rearrange("p (j e) -> p j e", e=1024)),
                "spill", 16)
            phase_sync((SY,))
        phase_sync()

    def emit_l0_ffn(bi):
        NBLK = T // 256
        hq = b.BIGB[:, 8192:16384]
        hqv = hq.rearrange("p (o t) -> p o t", t=256)
        for blk in range(NBLK):
            tsl = slice(blk * 256, blk * 256 + 256)
            phase_sync()
            xnb = b.SCF[:, 0:2048].rearrange("p (c t) -> p c t", t=256)
            ln_apply_block(xnb, tsl, 256)
            if blk == 0:
                tap("l0_xnf", [(b.SCF[:, 0:2048], 2048)])
            xq = b.WS[:, 2048:4096]
            qck = q_chain(b.SCF[:, 0:2048], BBF[:, 0:2048], BBF[:, 2048:4096],
                          xq, 3, -6)
            if blk == 0:
                tap("l0_xq", [(xq, 2048)])
            xqv = xq.rearrange("p (c t) -> p c t", t=256)
            b.wait(TE, qck)
            Hh3 = BAF.rearrange("p (o t) -> p o t", t=256)
            slab_g = [b.ck("pe"), b.ck("pe")]
            for oc in range(32):
                slab = b.WS[:, (oc % 2) * 1024:(oc % 2) * 1024 + 1024]
                slab3 = slab.rearrange("p (c e) -> p c e", e=128)
                b.wait(G, slab_g[oc % 2])
                lck = ld(slab3, w1q_d[:, :, oc * 128:oc * 128 + 128]
                         .rearrange("c p e -> p c e"), "ws%d" % (oc % 2))
                b.wait(TE, lck)
                pt2 = (b.PS4, b.PS5)[oc % 2]
                pname = ("ps4", "ps5")[oc % 2]
                b.guard(pname, TE)
                for c in range(NCH):
                    b.mm(pt2[:, 0:256], slab3[:, c, :], xqv[:, c, :],
                         start=(c == 0), stop=(c == NCH - 1))
                pec = b.mark(TE)
                slab_g[oc % 2] = pec
                b.wait(V, pec)
                b.ts(V, Hh3[:, oc, :], pt2[:, 0:256], bcol("l0_b1q", oc),
                     None, OP.add)
                b.setguard(pname, b.mark(V))
            for qt in range(4):
                Hq = BAF[:, qt * 2048:qt * 2048 + 2048]
                q_chain(Hq, BBF[:, 0:2048], BBF[:, 2048:4096], Hq, 10, -14)
                qck2 = q_chain(Hq, BBF[:, 0:2048], BBF[:, 2048:4096],
                               hq[:, qt * 2048:qt * 2048 + 2048],
                               3, -6, relu_first=True, exact_ties=True,
                               T3=b.SCF[:, 0:2048],
                               T4B=b.WS[:, 0:2048])
            if blk == 0:
                tap("l0_hq", [(hq[:, 0:4096], 4096)])
            b.wait(TE, qck2)
            ov = b.SCF[:, 0:2048].rearrange("p (c t) -> p c t", t=256)
            hs_g = [b.ck("ve"), b.ck("ve")]
            for oc in range(NCH):
                pt2 = (b.PS4, b.PS5)[oc % 2]
                pname = ("ps4", "ps5")[oc % 2]
                for half in range(2):
                    hbuf = b.WS[:, half * 2048:half * 2048 + 2048]
                    hb3 = hbuf.rearrange("p (c e) -> p c e", e=128)
                    b.wait(G, hs_g[half])
                    lck = ld(hb3, w2q_d[16 * half:16 * half + 16, :,
                                        oc * 128:oc * 128 + 128]
                             .rearrange("c p e -> p c e"), "wsh%d" % half)
                    b.wait(TE, lck)
                    if half == 0:
                        b.guard(pname, TE)
                    for c in range(16):
                        b.mm(pt2[:, 0:256], hb3[:, c, :],
                             hqv[:, 16 * half + c, :],
                             start=(half == 0 and c == 0),
                             stop=(half == 1 and c == 15))
                    hs_g[half] = b.mark(TE)
                pec = b.ck("pe")
                b.wait(V, pec)
                b.ts(V, ov[:, oc, :], pt2[:, 0:256], bcol("l0_b2q", oc),
                     None, OP.add)
                b.setguard(pname, b.mark(V))
            q510 = q_chain(b.SCF[:, 0:2048], BBF[:, 0:2048],
                           BBF[:, 2048:4096], b.SCF[:, 0:2048], 10, -14)
            if blk == 0:
                tap("l0_ffn_out", [(b.SCF[:, 0:2048], 2048)])
            mk = b.WS[:, 0:1024].bitcast(U8)
            b.wait(G, b.ck("pe"))
            b.wait(G, b.ck("ve"))
            mck = ld(mk.rearrange("p (c t) -> p c t", t=256),
                     W["l0_dropmask"][bi, :, :, tsl].rearrange(
                         "c p t -> p c t"), "mask")
            b.wait(V, mck)
            b.wait(V, q510)
            mkv = mk.rearrange("p (c t) -> p c t", t=256)
            for c in range(NCH):
                b.stt(V, BBF[:, 0:256], ov[:, c, :], 1.25,
                      mkv[:, c, :], OP.mult, OP.mult)
                b.tt(V, b.XR[:, c, tsl], b.XR[:, c, tsl],
                     BBF[:, 0:256], OP.add)
            b.mark(V)
        tap("l0_out", [(b.XR[:, c2, 0:256], 256) for c2 in range(NCH)])

    def emit_l1_attn(bi, ln_ck):
        SCFbf = b.SCF[:, :].bitcast(BF16)
        phase_sync()
        for c in range(NCH):
            b.tt(V, BA[:, c, 1:T], b.XN[:, c, 0:T - 1], b.XN[:, c, 1:T],
                 OP.subtract)
            b.ts(V, BA[:, c, 0:1], b.XN[:, c, 0:1], -1.0, None, OP.mult)
        b.mark(V)
        for c in range(NCH):
            b.stt(V, BB[:, c, :], BA[:, c, :], bcol("l1_tmx", c),
                  b.XN[:, c, :], OP.mult, OP.add)
        vmin = b.mark(V)
        mixdst = [SCFbf[:, 0:2048], SCFbf[:, 2048:4096],
                  b.WS[:, 2048:4096]]
        b.wait(TE, vmin)
        for piece in range(3):
            slab = b.WS[:, (piece % 2) * 512:(piece % 2) * 512 + 512]
            slab3 = slab.rearrange("p (c e) -> p c e", e=64)
            b.guard("l1mslab%d" % (piece % 2), G)
            lck = ld(slab3, W["l1_maaw1_swz"][piece], "ws%d" % (piece % 2))
            b.wait(TE, lck)
            for nb in range(4):
                tsl = slice(nb * 512, nb * 512 + 512)
                pt = (b.PS4, b.PS5)[nb % 2]
                pname = ("ps4", "ps5")[nb % 2]
                b.guard(pname, TE)
                for c in range(NCH):
                    b.mm(pt[0:64, 0:512], slab3[:, c, :], BB[:, c, tsl],
                         start=(c == 0), stop=(c == NCH - 1))
                pck = b.mark(TE)
                b.setguard("l1mslab%d" % (piece % 2), pck)
                b.wait(A, pck)
                b.act(mixdst[piece][0:64, tsl], pt[0:64, 0:512], AF.Tanh)
                b.setguard(pname, b.mark(A))
        PN = {"k": ("l1_tmk", 1, 1), "v": ("l1_tmv", 2, 2),
              "r": ("l1_tmr", 3, 0), "g": ("l1_tmg", 4, 4),
              "w": ("l1_tmw", 0, 3)}
        stg = b.WS[:, 4096:4608]
        stgf1 = b.WS[:, :].bitcast(F32)[:, 2048:2304]
        stgf2 = b.KVACC[:, 0:256]
        for p in ("k", "v", "r", "g", "w"):
            tmname, g_idx, slot = PN[p]
            r0 = 32 * (g_idx % 2)
            w2t = b.WS[:, 0:1024]
            b.guard("l1w2", G)
            b.wait(G, b.ck("pe"))
            b.wait(G, b.ck("act"))
            wname = ("l1_maaw2a", "l1_maaw2a", "l1_maaw2b", "l1_maaw2b",
                     "l1_maaw2c")[g_idx]
            wrow = (0, 32, 0, 32, 0)[g_idx]
            lck = ld(w2t[r0:r0 + 32, :],
                     W[wname][wrow:wrow + 32, :], "l1w2")
            b.wait(TE, lck)
            mixsrc = (mixdst[0], mixdst[0], mixdst[1], mixdst[1],
                      mixdst[2])[g_idx]
            mrow = r0
            for oc in range(NCH):
                for nb in range(4):
                    tsl = slice(nb * 512, nb * 512 + 512)
                    pt = (b.PS4, b.PS5)[nb % 2]
                    pname = ("ps4", "ps5")[nb % 2]
                    b.guard(pname, TE)
                    b.mm(pt[:, 0:512],
                         w2t[r0:r0 + 32, oc * 128:oc * 128 + 128],
                         mixsrc[mrow:mrow + 32, tsl],
                         tile_position=(r0, 0))
                    pck = b.mark(TE)
                    b.setguard("l1w2", pck)
                    b.wait(V, pck)
                    b.guard("ps6", V)
                    b.stt(V, b.PS6[:, 0:512], pt[:, 0:512], bcol(tmname, oc),
                          BA[:, oc, tsl], OP.add, OP.mult)
                    b.tt(V, BB[:, oc, tsl], b.PS6[:, 0:512],
                         b.XN[:, oc, tsl], OP.add)
                    vx = b.mark(V)
                    b.setguard("ps6", vx)
                    b.setguard(pname, vx)
            vxp = b.mark(V)
            b.wait(TE, vxp)
            if p != "w":
                wsw = W["l1_w%s_swz" % p]

                def evacp(oc, nb, ps, tsl, pe_ck, slot=slot, p=p):
                    b.wait(A, pe_ck)
                    for hf in range(2):
                        sg = (stgf1, stgf2)[hf]
                        hsl = slice(tsl.start + hf * 256,
                                    tsl.start + hf * 256 + 256)
                        b.guard("l1stg", A)
                        if p == "g":
                            b.act(sg, ps[:, hf * 256:hf * 256 + 256], AF.Silu)
                        else:
                            b.act(sg, ps[:, hf * 256:hf * 256 + 256],
                                  AF.Identity)
                        ack = b.mark(A)
                        b.inc(A.dma_start(l1s_d[slot, oc, :, hsl], sg),
                              "spill", 16)
                        b.setguard("l1stg", b.ck("spill"))
                    return ack
                proj_wstat(wsw, BB, NCH, evacp, wsoff=1024)
            else:
                slab = b.WS[:, 0:512]
                slab3 = slab.rearrange("p (c e) -> p c e", e=64)
                b.guard("l1w2", G)
                b.wait(G, b.ck("pe"))
                lck = ld(slab3, W["l1_tdw1_swz"][0], "ws0")
                b.wait(TE, lck)
                tb = b.WS[:, 2048:4096]
                for nb in range(4):
                    tsl = slice(nb * 512, nb * 512 + 512)
                    pt = (b.PS4, b.PS5)[nb % 2]
                    pname = ("ps4", "ps5")[nb % 2]
                    b.guard(pname, TE)
                    for c in range(NCH):
                        b.mm(pt[0:64, 0:512], slab3[:, c, :], BB[:, c, tsl],
                             start=(c == 0), stop=(c == NCH - 1))
                    pck = b.mark(TE)
                    b.wait(A, pck)
                    b.act(tb[0:64, tsl], pt[0:64, 0:512], AF.Tanh)
                    b.setguard(pname, b.mark(A))
                atb = b.ck("act")
                tdw2 = b.WS[:, 1024:2048]
                b.wait(G, b.ck("pe"))
                lck = ld(tdw2[0:64, :], W["l1_tdw2"], "l1w2")
                b.wait(TE, lck)
                b.wait(TE, atb)
                for oc in range(NCH):
                    for nb in range(4):
                        tsl = slice(nb * 512, nb * 512 + 512)
                        pt = (b.PS4, b.PS5)[nb % 2]
                        pname = ("ps4", "ps5")[nb % 2]
                        b.guard(pname, TE)
                        b.mm(pt[:, 0:512],
                             tdw2[0:64, oc * 128:oc * 128 + 128],
                             tb[0:64, tsl])
                        pck = b.mark(TE)
                        b.wait(A, pck)
                        for hf in range(2):
                            sg = (stgf1, stgf2)[hf]
                            hsl = slice(tsl.start + hf * 256,
                                        tsl.start + hf * 256 + 256)
                            b.guard("l1stg", A)
                            b.act(sg, pt[:, hf * 256:hf * 256 + 256], AF.Exp,
                                  bias=bcol("l1_tdec", oc))
                            b.mark(A)
                            b.inc(A.dma_start(l1s_d[slot, oc, :, hsl], sg),
                                  "spill", 16)
                            b.setguard("l1stg", b.ck("spill"))
                        ack = b.mark(A)
                        b.setguard(pname, ack)
        phase_sync()

        S = b.KVACC[:, 0:512]
        b.em(V, [], [S])
        V.memset(S, 0.0)
        b.mark(V)
        LTM64 = b.CST[0:64, 128:192]
        CS = b.SCF[:, 0:512].rearrange("p (c e) -> p c e", e=64)
        TA = b.SCF[:, 512:576]
        TB = b.SCF[:, 576:640]
        TU = b.SCF[0:64, 640:704]
        rPs = b.SCF[:, 704:768]
        kQs = b.SCF[:, 768:832]
        ms = b.SCF[:, 832:896]
        kQT = b.SCF[:, 896:1024]
        vT = b.SCF[:, 1024:1152]
        sTs = b.SCF[:, 1152:1216]
        Ps = b.SCF[:, 1216:1280]
        ZAP = b.SCF[:, 1280:1344]
        ystg = b.PSB[0:64, 0:1024]
        SQH = b.PSB[0:64, 1024:1536]
        PTF = b.PT[:, :].bitcast(F32)
        WSF = b.WS[:, :].bitcast(F32)
        rblk = BAF[:, 0:2048].rearrange("p (c e) -> p c e", e=256)
        kblk = BAF[:, 2048:4096].rearrange("p (c e) -> p c e", e=256)
        vblk = BAF[:, 4096:6144].rearrange("p (c e) -> p c e", e=256)
        eblk = BAF[:, 6144:8192].rearrange("p (c e) -> p c e", e=256)
        gblk = WSF[:, 0:2048].rearrange("p (c e) -> p c e", e=256)
        NCK = T // 64
        for cc in range(NCK):
            if cc % 4 == 0:
                tsl = slice((cc // 4) * 256, (cc // 4) * 256 + 256)
                b.guard("l1blk", SY)
                b.guard("l1blk", G)
                for oc in range(NCH):
                    ldh(rblk[:, oc, :], l1s_d[0, oc, :, tsl], "l1r")
                    ldh(kblk[:, oc, :], l1s_d[1, oc, :, tsl], "l1k")
                    ldh(vblk[:, oc, :], l1s_d[2, oc, :, tsl], "l1v")
                    ldh(eblk[:, oc, :], l1s_d[3, oc, :, tsl], "l1e")
                    b.inc(G.dma_start(gblk[:, oc, :], l1s_d[4, oc, :, tsl]),
                          b.dsem("l1g"), 16)
                for kk in ("l1r", "l1k", "l1v", "l1e", "l1g"):
                    b.wait(V, b.ck("d_" + kk))
                    b.wait(TE, b.ck("d_" + kk))
            if cc == 0:
                tap("l1_rkve", [(rblk[:, oc2, 0:256], 256) for oc2 in range(8)]
                    + [(kblk[:, oc2, 0:256], 256) for oc2 in range(8)]
                    + [(vblk[:, oc2, 0:256], 256) for oc2 in range(8)]
                    + [(eblk[:, oc2, 0:256], 256) for oc2 in range(8)])
            co = (cc % 4) * 64
            csl = slice(co, co + 64)
            b.guard("ps4", TE)
            b.guard("l1cs", V)
            for oc in range(NCH):
                b.scan(CS[:, oc, :], eblk[:, oc, csl], eblk[:, oc, csl],
                       0.0, OP.add, OP.bypass)
            vcs = b.mark(V)
            ystart = b.ck("ve")
            for oc in range(NCH):
                b.tt(V, TA, eblk[:, oc, csl], CS[:, oc, :], OP.subtract)
                va = b.mark(V)
                b.wait(A, va)
                b.act(TA, TA, AF.Exp)
                b.act(TB, CS[:, oc, :], AF.Exp)
                aab = b.mark(A)
                b.wait(V, aab)
                b.tt(V, rPs, rblk[:, oc, csl], TA, OP.mult)
                b.tt(V, kQs, kblk[:, oc, csl], TB, OP.mult)
                b.stt(V, ms, rblk[:, oc, csl], bcol("l1_ucol", oc),
                      kblk[:, oc, csl], OP.mult, OP.mult)
                b.copy(V, ZAP, vblk[:, oc, csl])
                b.red(V, b.COLS[:, oc:oc + 1], eblk[:, oc, csl], OP.add)
                vstage = b.mark(V)
                b.wait(TE, vstage)
                b.guard("pt", TE)
                b.tr(PTF[0:64, 0:128], kQs, b.IDENF[:, :])
                b.tr(PTF[0:64, 128:256], ZAP, b.IDENF[:, :])
                ptk = b.mark(TE)
                b.wait(V, ptk)
                b.copy(V, kQT[0:64, :], PTF[0:64, 0:128])
                b.copy(V, vT[0:64, :], PTF[0:64, 128:256])
                vtr = b.mark(V)
                b.setguard("pt", vtr)
                b.wait(TE, vtr)
                for hh in range(2):
                    h = 2 * oc + hh
                    pr = 64 * hh
                    b.guard("ps6", TE)
                    b.mm(b.PS6[0:64, 0:64], rPs[pr:pr + 64, :],
                         kQs[pr:pr + 64, :], tile_position=(pr, 0))
                    b.mm(b.PS6[0:64, 64:65], ms[pr:pr + 64, :],
                         b.ONECF[pr:pr + 64, :], tile_position=(pr, 0))
                    psc = b.mark(TE)
                    b.wait(V, psc)
                    b.tt(V, Ps[0:64, :], b.PS6[0:64, 0:64], LTM64, OP.mult)
                    vp = b.mark(V)
                    b.wait(TE, vp)
                    b.guard("pt", TE)
                    b.tr(PTF[0:64, 256:320], Ps[0:64, :], b.IDENF[0:64, 0:64])
                    ptp = b.mark(TE)
                    b.wait(V, ptp)
                    b.copy(V, sTs[0:64, :], PTF[0:64, 256:320])
                    vst = b.mark(V)
                    b.setguard("pt", vst)
                    b.wait(TE, vst)
                    b.guard("ps5", TE)
                    b.mm(b.PS5[0:64, 0:64], sTs[0:64, :], vT[0:64, pr:pr + 64],
                         start=True, stop=False)
                    b.mm(b.PS5[0:64, 0:64], rPs[pr:pr + 64, :],
                         S[pr:pr + 64, oc * 64:oc * 64 + 64],
                         start=False, stop=True, tile_position=(pr, 0))
                    b.mm(b.PS4[pr:pr + 64, oc * 64:oc * 64 + 64],
                         kQT[0:64, pr:pr + 64], vT[0:64, pr:pr + 64],
                         tile_position=(0, pr))
                    pey = b.mark(TE)
                    b.wait(V, pey)
                    b.ts(V, TU, vT[0:64, pr:pr + 64], b.PS6[0:64, 64:65],
                         None, OP.mult)
                    b.tt(V, ystg[0:64, h * 64:h * 64 + 64],
                         b.PS5[0:64, 0:64], TU, OP.add)
                    vy = b.mark(V)
                    b.setguard("ps5", vy)
                    b.setguard("ps6", vy)
            vdone = b.mark(V)
            b.wait(A, vdone)
            b.act(b.COLS[:, 8:16], b.COLS[:, 0:8], AF.Exp, scale=-1.0)
            aexp = b.mark(A)
            b.wait(V, aexp)
            for oc in range(NCH):
                sl = slice(oc * 64, oc * 64 + 64)
                b.tt(V, S[:, sl], S[:, sl], b.PS4[:, sl], OP.add)
                b.ts(V, S[:, sl], S[:, sl], b.COLS[:, 8 + oc:9 + oc], None,
                     OP.mult)
            vsu = b.mark(V)
            b.setguard("ps4", vsu)
            b.setguard("l1cs", vsu)
            y3 = ystg.rearrange("p (g e) -> p g e", e=64)
            b.red(V, b.COLS[0:64, 16:32], y3, OP.add)
            b.ts(V, b.COLS[0:64, 16:32], b.COLS[0:64, 16:32], 1.0 / 64,
                 None, OP.mult)
            vm = b.mark(V)
            for hf in range(2):
                b.wait(A, b.ck("ve"))
                b.act(SQH, ystg[0:64, hf * 512:hf * 512 + 512], AF.Square)
                asq = b.mark(A)
                b.wait(V, asq)
                sq3 = SQH.rearrange("p (g e) -> p g e", e=64)
                b.red(V, b.COLS[0:64, 32 + 8 * hf:40 + 8 * hf], sq3, OP.add)
                b.mark(V)
            b.ts(V, b.COLS[0:64, 32:48], b.COLS[0:64, 32:48], 1.0 / 64,
                 None, OP.mult)
            b.tt(V, b.COLS[0:64, 48:64], b.COLS[0:64, 16:32],
                 b.COLS[0:64, 16:32], OP.mult)
            b.tt(V, b.COLS[0:64, 32:48], b.COLS[0:64, 32:48],
                 b.COLS[0:64, 48:64], OP.subtract)
            b.ts(V, b.COLS[0:64, 32:48], b.COLS[0:64, 32:48], 6.4e-4, None,
                 OP.add)
            b.copy(V, b.COLS[0:64, 0:16], b.COLS[0:64, 32:48])
            b.recip(b.COLS[0:64, 48:64], b.COLS[0:64, 32:48])
            vv2 = b.mark(V)
            b.wait(A, vv2)
            b.act(b.COLS[0:64, 32:48], b.COLS[0:64, 48:64], AF.Sqrt)
            ar = b.mark(A)
            b.wait(V, ar)
            b.tt(V, b.COLS[0:64, 48:64], b.COLS[0:64, 32:48],
                 b.COLS[0:64, 32:48], OP.mult)
            b.tt(V, b.COLS[0:64, 48:64], b.COLS[0:64, 48:64],
                 b.COLS[0:64, 0:16], OP.mult)
            b.ts(V, b.COLS[0:64, 48:64], b.COLS[0:64, 48:64], -0.5, 1.5,
                 OP.mult, OP.add)
            b.tt(V, b.COLS[0:64, 32:48], b.COLS[0:64, 32:48],
                 b.COLS[0:64, 48:64], OP.mult)
            mb16 = b.COLS[0:64, 16:32].unsqueeze(-1).broadcast_to([64, 16, 64])
            rb16 = b.COLS[0:64, 32:48].unsqueeze(-1).broadcast_to([64, 16, 64])
            b.tt(V, y3, y3, mb16, OP.subtract)
            b.tt(V, y3, y3, rb16, OP.mult)
            vyn = b.mark(V)
            b.wait(TE, vyn)
            tpos = cc * 64
            yf = b.SCF[0:64, 1760:1888]
            for oc in range(NCH):
                b.guard("yf", V)
                b.copy(V, yf, ystg[0:64, oc * 128:oc * 128 + 128])
                vcp = b.mark(V)
                b.wait(TE, vcp)
                b.guard("pt", TE)
                b.tr(PTF[:, 320:384], yf, b.IDENF[0:64, 0:64])
                ptz = b.mark(TE)
                b.setguard("yf", ptz)
                b.wait(V, ptz)
                b.ts(V, ZAP, PTF[:, 320:384],
                     bcol("l1_lng", oc), bcol("l1_lnb", oc), OP.mult, OP.add)
                b.tt(V, BB[:, oc, tpos:tpos + 64], ZAP,
                     gblk[:, oc, csl], OP.mult)
                b.setguard("pt", b.mark(V))
            b.setguard("l1blk", b.ck("ve"))
        tap("l1_z", [(BB[:, c2, 0:256], 256) for c2 in range(NCH)])
        phase_sync()

        def evaco(oc, nb, ps, tsl, pe_ck):
            b.wait(V, pe_ck)
            b.copy(V, b.SCF[:, 0:512], ps)
            b.tt(V, b.XR[:, oc, tsl], b.XR[:, oc, tsl], b.SCF[:, 0:512],
                 OP.add)
            return b.mark(V)
        proj_wstat(W["l1_wo_swz"], BB, NCH, evaco)
        tap("l1_out", [(b.XR[:, c2, 0:256], 256) for c2 in range(NCH)])

    def emit_l1_ffn(bi):
        hgf3 = BBF[:, 0:8192].rearrange("p (o t) -> p o t", t=256)
        NBLK = T // 256
        for blk in range(NBLK):
            tsl = slice(blk * 256, blk * 256 + 256)
            phase_sync()
            Hh3 = BAF.rearrange("p (o t) -> p o t", t=256)
            WSFf = b.WS[:, :].bitcast(F32)
            xn32 = b.SCF[:, 0:2048].rearrange("p (c t) -> p c t", t=256)
            lnv = ln_apply_block(xn32, tsl, 256)
            b.wait(TE, lnv)
            slab_g = [b.ck("pe"), b.ck("pe")]
            for oc in range(32):
                slab = WSFf[:, (oc % 2) * 1024:(oc % 2) * 1024 + 1024]
                slab3 = slab.rearrange("p (c e) -> p c e", e=128)
                b.wait(G, slab_g[oc % 2])
                lck = ld(slab3, W["l1f_w1_swz"][oc], "ws%d" % (oc % 2))
                b.wait(TE, lck)
                pt2 = (b.PS4, b.PS5)[oc % 2]
                pname = ("ps4", "ps5")[oc % 2]
                b.guard(pname, TE)
                for c in range(NCH):
                    b.mm(pt2[:, 0:256], slab3[:, c, :], xn32[:, c, :],
                         start=(c == 0), stop=(c == NCH - 1))
                pec = b.mark(TE)
                slab_g[oc % 2] = pec
                b.wait(V, pec)
                b.ts(V, Hh3[:, oc, :], pt2[:, 0:256], bcol("l1f_b1", oc),
                     None, OP.add)
                b.setguard(pname, b.mark(V))
            if blk == 0:
                tap("l1f_xn", [(xn32[:, c, :], 256) for c in range(NCH)])
                tap("l1f_h", [(Hh3[:, o, :], 256) for o in range(32)])
            for qt in range(4):
                Hq = BAF[:, qt * 2048:qt * 2048 + 2048]
                T1, T2 = {
                    0: (BBF[:, 2048:4096], BBF[:, 4096:6144]),
                    1: (BBF[:, 4096:6144], BBF[:, 6144:8192]),
                    2: (BBF[:, 6144:8192], BAF[:, 0:2048]),
                    3: (BAF[:, 0:2048], BAF[:, 2048:4096])}[qt]
                T3 = b.SCF[:, 0:2048]
                rne2(T1, Hq, 256.0, 0.0)
                b.ts(V, T2, T1, 1.0 / 64, 16.0, OP.mult, OP.add)
                b.ts(V, T2, T2, -0.5, None, OP.add)
                rne2(Hq, T2, 1.0, 0.0)
                b.ts(V, Hq, Hq, -1.0, 32.0, OP.max, OP.min)
                b.ts(V, T2, Hq, 0.25, -4.0, OP.mult, OP.add)
                vck = b.mark(V)
                b.wait(A, vck)
                b.act(T3, T2, AF.Gelu)
                b.act(Hq, T2, AF.Gelu, bias=ccol(0.25))
                ag = b.mark(A)
                b.wait(V, ag)
                b.tt(V, Hq, Hq, T3, OP.subtract)
                b.stt(V, T2, T1, 1.0 / 256, T2, OP.mult, OP.subtract)
                b.stt(V, Hq, T2, 4.0, Hq, OP.mult, OP.mult)
                b.tt(V, T3, T3, Hq, OP.add)
                f2f8(BBF[:, qt * 2048:qt * 2048 + 2048], T3, T1, T2)
            qck2 = b.ck("ve")
            if blk == 0:
                tap("l1f_hg", [(hgf3[:, o, :], 256) for o in range(32)])
            b.wait(TE, qck2)
            ov = b.SCF[:, 0:2048].rearrange("p (c t) -> p c t", t=256)
            hs_g = [b.ck("ve"), b.ck("ve")]
            for oc in range(NCH):
                pt2 = (b.PS4, b.PS5)[oc % 2]
                pname = ("ps4", "ps5")[oc % 2]
                for half in range(4):
                    hbuf = WSFf[:, (half % 2) * 1024:(half % 2) * 1024 + 1024]
                    hb3 = hbuf.rearrange("p (c e) -> p c e", e=128)
                    b.wait(G, hs_g[half % 2])
                    lck = ld(hb3, W["l1f_w2_swz"][oc][:, 8 * half:8 * half + 8, :],
                             "wsh%d" % (half % 2))
                    b.wait(TE, lck)
                    if half == 0:
                        b.guard(pname, TE)
                    for c in range(8):
                        b.mm(pt2[:, 0:256], hb3[:, c, :],
                             hgf3[:, 8 * half + c, :],
                             start=(half == 0 and c == 0),
                             stop=(half == 3 and c == 7))
                    hs_g[half % 2] = b.mark(TE)
                pec = b.ck("pe")
                b.wait(V, pec)
                b.ts(V, ov[:, oc, :], pt2[:, 0:256], bcol("l1f_b2", oc),
                     None, OP.add)
                b.setguard(pname, b.mark(V))
            f2f8(b.SCF[:, 0:2048], b.SCF[:, 0:2048], BBF[:, 0:2048],
                 BBF[:, 2048:4096])
            for c in range(NCH):
                b.tt(V, b.XR[:, c, tsl], b.XR[:, c, tsl], ov[:, c, :], OP.add)
            b.mark(V)
        tap("l1f_out", [(b.XR[:, c2, 0:256], 256) for c2 in range(NCH)])

    def emit_l2_attn(bi, ln_ck):
        SCFbf = b.SCF[:, :].bitcast(BF16)
        phase_sync()
        def evk(oc, nb, ps, tsl, pe_ck):
            b.wait(V, pe_ck)
            b.ts(V, BA[:, oc, tsl], ps, bcol("l2_bk", oc), None, OP.add)
            return b.mark(V)
        proj_wstat(W["l2_wk_swz"], b.XN[:, :, :], NCH, evk)
        vtok = b.BIGB[:, :].rearrange("p (tt e) -> p tt e", e=1024)

        def evv(oc, nb, ps, tsl, pe_ck):
            b.wait(V, pe_ck)
            stg = SCFbf[:, 0:512]
            b.guard("l2vstg", V)
            b.ts(V, stg, ps, bcol("l2_bv", oc), None, OP.add)
            vs = b.mark(V)
            b.wait(TE, vs)
            b.guard("pt", TE)
            for j in range(4):
                b.tr(b.PT[:, j * 128:j * 128 + 128],
                     stg[:, j * 128:j * 128 + 128], IDB)
            pck = b.mark(TE)
            b.wait(V, pck)
            for j in range(4):
                tt2 = nb * 4 + j
                b.copy(V, vtok[:, tt2, oc * 128:oc * 128 + 128],
                       b.PT[:, j * 128:j * 128 + 128])
            vc = b.mark(V)
            b.setguard("pt", vc)
            b.setguard("l2vstg", vc)
            return vc
        proj_wstat(W["l2_wv_swz"], b.XN[:, :, :], NCH, evv)
        phase_sync()
        qts = SCFbf[:, 0:4096].rearrange("p (c e) -> p c e", e=512)
        for qb in range(4):
            tslq = slice(qb * 512, qb * 512 + 512)
            slab_g = [b.ck("pe"), b.ck("pe")]
            for oc in range(NCH):
                slab = b.WS[:, (oc % 2) * 1024:(oc % 2) * 1024 + 1024]
                slab3 = slab.rearrange("p (c e) -> p c e", e=128)
                b.wait(G, slab_g[oc % 2])
                b.wait(G, b.ck("ve"))
                lck = ld(slab3, W["l2_wq_swz"][oc], "ws%d" % (oc % 2))
                b.wait(TE, lck)
                pt2 = (b.PS4, b.PS5)[oc % 2]
                pname = ("ps4", "ps5")[oc % 2]
                b.guard(pname, TE)
                for c in range(NCH):
                    b.mm(pt2[:, 0:512], slab3[:, c, :], b.XN[:, c, tslq],
                         start=(c == 0), stop=(c == NCH - 1))
                pec = b.mark(TE)
                slab_g[oc % 2] = pec
                b.wait(V, pec)
                b.ts(V, qts[:, oc, :], pt2[:, 0:512], bcol("l2_bq", oc),
                     None, OP.add)
                b.setguard(pname, b.mark(V))
            vq = b.mark(V)
            b.wait(TE, vq)
            for j in range(4):
                qt = qb * 4 + j
                kw = (qt + 1) * 128
                b.guard("psb", TE)
                nseg = (kw + 511) // 512
                for c in range(NCH):
                    for sg in range(nseg):
                        s0 = sg * 512
                        sw = min(512, kw - s0)
                        b.mm(b.PSB[:, s0:s0 + sw],
                             qts[:, c, j * 128:j * 128 + 128],
                             BA[:, c, s0:s0 + sw],
                             start=(c == 0), stop=(c == NCH - 1))
                psc = b.mark(TE)
                b.wait(V, psc)
                dg = slice(kw - 128, kw)
                b.tt(V, b.SCF[:, 2048:2176], b.PSB[:, dg], b.CAUS[:, :],
                     OP.add)
                b.copy(V, b.PSB[:, dg], b.SCF[:, 2048:2176])
                b.drainall(V)
                b.red(V, b.COLS[:, 0:1], b.PSB[:, 0:kw], OP.max, negate=True)
                b.ts(V, b.COLS[:, 1:2], b.COLS[:, 0:1], 1.0 / 32, None,
                     OP.mult)
                vmx = b.mark(V)
                b.wait(A, vmx)
                Pw = b.WS[:, 2048:2048 + kw]
                b.act(Pw, b.PSB[:, 0:kw], AF.Exp, scale=1.0 / 32,
                      bias=b.COLS[:, 1:2], accum_out=b.COLS[:, 2:3])
                aex = b.mark(A)
                b.setguard("psb", aex)
                b.wait(V, aex)
                b.recip(b.COLS[:, 3:4], b.COLS[:, 2:3])
                vrc = b.mark(V)
                b.wait(TE, aex)
                b.guard("ps4", TE)
                b.guard("ps5", TE)
                for kb in range(qt + 1):
                    b.guard("pt", TE)
                    b.tr(b.PT[:, 0:128], Pw[:, kb * 128:kb * 128 + 128], IDB)
                    ptc = b.mark(TE)
                    b.wait(V, ptc)
                    b.copy(V, SCFbf[:, 4096:4224], b.PT[:, 0:128])
                    vpt = b.mark(V)
                    b.setguard("pt", vpt)
                    b.wait(TE, vpt)
                    b.mm(b.PS4[:, :], SCFbf[:, 4096:4224],
                         vtok[:, kb, 0:512],
                         start=(kb == 0), stop=(kb == qt))
                    b.mm(b.PS5[:, :], SCFbf[:, 4096:4224],
                         vtok[:, kb, 512:1024],
                         start=(kb == 0), stop=(kb == qt))
                pey = b.mark(TE)
                b.wait(V, pey)
                b.wait(V, vrc)
                ybf = b.WS[:, 0:1024]
                b.guard("l2ybf", V)
                b.ts(V, ybf[:, 0:512], b.PS4[:, :], b.COLS[:, 3:4], None,
                     OP.mult)
                b.ts(V, ybf[:, 512:1024], b.PS5[:, :], b.COLS[:, 3:4], None,
                     OP.mult)
                vyb = b.mark(V)
                b.setguard("ps4", vyb)
                b.setguard("ps5", vyb)
                b.wait(TE, vyb)
                for oc in range(NCH):
                    b.guard("pt", TE)
                    b.tr(b.PT[:, 0:128], ybf[:, oc * 128:oc * 128 + 128], IDB)
                    ptc = b.mark(TE)
                    b.wait(V, ptc)
                    b.tt(V, b.XR[:, oc, qt * 128:qt * 128 + 128],
                         b.XR[:, oc, qt * 128:qt * 128 + 128],
                         b.PT[:, 0:128], OP.add)
                    vxr = b.mark(V)
                    b.setguard("pt", vxr)
                b.setguard("l2ybf", b.ck("ve"))
        tap("l2_out", [(b.XR[:, c2, 0:256], 256) for c2 in range(NCH)])

    def emit_l2_ffn(bi):
        phase_sync()
        NBLK2 = 4

        def tproj(wname, scname, src3, noc, src_nch, actf, dst_cb):
            if scname in ("l2f_s1", "l2f_s2", "l2f_sh1", "l2f_sh2"):
                b.wait(SY, b.ck("ve"))
                b.wait(SY, b.ck("act"))
                sck = ldh(b.COLS[:, 32:64], W[scname], "l2sc")
                b.wait(V, sck)
                b.wait(A, sck)
                scol = lambda oc: b.COLS[:, 32 + oc:33 + oc]
            else:
                scol = lambda oc: bcol(scname, oc)
            if src_nch == NCH:
                slab_g = [b.ck("pe"), b.ck("pe")]
                for oc in range(noc):
                    slab = b.WS[:, (oc % 2) * 1024:(oc % 2) * 1024 + 1024]
                    slab3 = slab.rearrange("p (c e) -> p c e", e=128)
                    b.wait(G, slab_g[oc % 2])
                    b.wait(G, b.ck("ve"))
                    lck = ld(slab3, W[wname][oc], "ws%d" % (oc % 2))
                    b.wait(TE, lck)
                    pt2 = (b.PS4, b.PS5)[oc % 2]
                    pname = ("ps4", "ps5")[oc % 2]
                    b.guard(pname, TE)
                    for c in range(src_nch):
                        b.mm(pt2[:, 0:512], slab3[:, c, :], src3[:, c, :],
                             start=(c == 0), stop=(c == src_nch - 1))
                    pec = b.mark(TE)
                    slab_g[oc % 2] = pec
                    cck = dst_cb(oc, pt2[:, 0:512], pec, actf, scol)
                    b.setguard(pname, cck)
            else:
                hs_g = [b.ck("ve"), b.ck("ve")]
                for oc in range(noc):
                    pt2 = (b.PS4, b.PS5)[oc % 2]
                    pname = ("ps4", "ps5")[oc % 2]
                    for half in range(2):
                        hbuf = b.WS[:, half * 2048:half * 2048 + 2048]
                        hb3 = hbuf.rearrange("p (c e) -> p c e", e=128)
                        b.wait(G, hs_g[half])
                        b.wait(G, b.ck("ve"))
                        lck = ld(hb3, W[wname][oc][:, 16 * half:16 * half + 16, :],
                                 "wsh%d" % half)
                        b.wait(TE, lck)
                        if half == 0:
                            b.guard(pname, TE)
                        for c in range(16):
                            b.mm(pt2[:, 0:512], hb3[:, c, :],
                                 src3[:, 16 * half + c, :],
                                 start=(half == 0 and c == 0),
                                 stop=(half == 1 and c == 15))
                        hs_g[half] = b.mark(TE)
                    pec = b.ck("pe")
                    cck = dst_cb(oc, pt2[:, 0:512], pec, actf, scol)
                    b.setguard(pname, cck)

        for nb in range(NBLK2):
            tsl = slice(nb * 512, nb * 512 + 512)
            xnb = b.XN[:, :, tsl]
            ba3 = b.BIGA[:, :].rearrange("p (o e) -> p o e", e=512)
            bb3 = b.BIGB[:, :].rearrange("p (o e) -> p o e", e=512)

            def cb_t2(oc, ps, pec, actf, scol):
                b.wait(V, pec)
                b.ts(V, bb3[:, oc, :], ps, scol(oc), None, OP.mult)
                return b.mark(V)
            tproj("l2f_w2_swz", "l2f_s2", xnb, 32, NCH, None, cb_t2)
            def cb_h(oc, ps, pec, actf, scol):
                b.wait(A, pec)
                b.act(b.SCF[:, 0:512], ps, AF.Silu, scale=scol(oc))
                ak = b.mark(A)
                b.wait(V, ak)
                b.tt(V, ba3[:, oc, :], b.SCF[:, 0:512], bb3[:, oc, :],
                     OP.mult)
                return b.mark(V)
            tproj("l2f_w1_swz", "l2f_s1", xnb, 32, NCH, None, cb_h)
            def cb_g(dst3):
                def f(oc, ps, pec, actf, scol):
                    b.wait(A, pec)
                    b.act(b.SCF[:, 0:512], ps, AF.Gelu, scale=scol(oc))
                    ak = b.mark(A)
                    b.wait(V, ak)
                    b.copy(V, dst3[:, oc, :], b.SCF[:, 0:512])
                    return b.mark(V)
                return f
            tproj("l2f_h1_swz", "l2f_sh1", ba3, 32, 32, None, cb_g(bb3))
            tproj("l2f_h2_swz", "l2f_sh2", bb3, 32, 32, None, cb_g(ba3))
            def cb_o(oc, ps, pec, actf, scol):
                b.wait(V, pec)
                b.ts(V, b.SCF[:, 0:512], ps, scol(oc), None, OP.mult)
                b.tt(V, b.XR[:, oc, tsl], b.XR[:, oc, tsl], b.SCF[:, 0:512],
                     OP.add)
                return b.mark(V)
            tproj("l2f_w3_swz", "l2f_s3", ba3, NCH, 32, None, cb_o)
            phase_sync()
        tap("l2f_out", [(b.XR[:, c2, 0:256], 256) for c2 in range(NCH)])

    def emit_l3_xn2(bi, ln_ck):
        phase_sync()
        for c in range(NCH):
            for nb2 in range(T // 512):
                sl = slice(nb2 * 512, nb2 * 512 + 512)
                b.mm(b.PSB[0:1, sl], ONEC, b.XN[:, c, sl],
                     start=(c == 0), stop=(c == NCH - 1))
        pe1 = b.mark(TE)
        b.wait(V, pe1)
        b.ts(V, b.ROW0T[:, :], b.PSB[0:1, 0:T], 1.0 / C, None, OP.mult)
        vs1 = b.mark(V)
        b.wait(TE, vs1)
        sq = b.WS[:, 0:T]
        TR = b.SCF[0:1, 0:T]
        TR2 = b.WS[:, :].bitcast(F32)[0:1, 0:T]
        for c in range(NCH):
            b.guard("lnsq", A)
            b.act(sq, b.XN[:, c, :], AF.Square)
            ack = b.mark(A)
            b.wait(TE, ack)
            for nb2 in range(T // 512):
                sl = slice(nb2 * 512, nb2 * 512 + 512)
                b.mm(b.PSB[0:1, sl], ONEC, sq[:, sl],
                     start=(c == 0), stop=(c == NCH - 1))
            b.setguard("lnsq", b.mark(TE))
        pe2 = b.ck("pe")
        b.wait(V, pe2)
        b.ts(V, TR, b.PSB[0:1, 0:T], 1.0 / C, None, OP.mult)
        b.tt(V, TR2, b.ROW0T[:, :], b.ROW0T[:, :], OP.mult)
        b.tt(V, TR, TR, TR2, OP.subtract)
        b.ts(V, TR, TR, 1e-5, None, OP.add)
        TR3 = b.PSB[0:1, 0:T]
        b.copy(V, TR3, TR)
        b.recip(TR2, TR)
        v2 = b.mark(V)
        b.wait(A, v2)
        b.act(TR, TR2, AF.Sqrt)
        a2 = b.mark(A)
        b.wait(V, a2)
        b.tt(V, TR2, TR, TR, OP.mult)
        b.tt(V, TR2, TR2, TR3, OP.mult)
        b.ts(V, TR2, TR2, -0.5, 1.5, OP.mult, OP.add)
        b.tt(V, TR, TR, TR2, OP.mult)
        b.tt(V, b.ROW2[:, :], b.ROW2[:, :], b.ROW0T[:, :], OP.subtract)
        b.tt(V, b.ROW2[:, :], b.ROW2[:, :], TR, OP.mult)
        b.tt(V, b.ROW1[:, :], b.ROW1[:, :], TR, OP.mult)
        v3 = b.mark(V)
        b.wait(TE, v3)
        for nb2 in range(T // 512):
            sl = slice(nb2 * 512, nb2 * 512 + 512)
            b.guard("ps4", TE)
            b.guard("ps5", TE)
            b.mm(b.PS4[:, :], b.ONESR[:, :], b.ROW1[:, sl])
            b.mm(b.PS5[:, :], b.ONESR[:, :], b.ROW2[:, sl])
            pb = b.mark(TE)
            b.wait(V, pb)
            for c in range(NCH):
                st = b.SCF[:, (c % 4) * 512:(c % 4) * 512 + 512]
                b.guard("l3x2s", V)
                b.tt(V, st, b.XR[:, c, sl], b.PS4[:, :], OP.mult)
                b.tt(V, st, st, b.PS5[:, :], OP.add)
                b.ts(V, st, st, bcol("l3_lnw", c), bcol("l3_lnb", c),
                     OP.mult, OP.add)
                vw = b.mark(V)
                b.wait(A, vw)
                b.inc(A.dma_start(l3x2_d[c, :, sl], st), b.dsem("l3x2"), 16)
                b.setguard("l3x2s", b.ck("d_l3x2"))
            vm = b.mark(V)
            b.setguard("ps4", vm)
            b.setguard("ps5", vm)
        tap("l3_rows", [(b.ROW1[0:1, 0:T], T), (b.ROW2[0:1, 0:T], T)])

    def emit_l3_ffn(bi):
        phase_sync()
        wgt = b.WS[:, 4552:4616]
        lck0 = ld(wgt.rearrange("p (c e) -> p c e", e=8),
                  W["l3f_wg"].rearrange("(c p) e -> p c e", p=128), "l3w")
        wev = b.BIGA[:, 0:8192].rearrange("p (c e) -> p c e", e=1024)
        for c in range(NCH):
            ld(wev[:, c, :], W["l3f_we"][128 * c:128 * c + 128, :], "l3we")
        beB = b.BIGA[:, 8192:9216]
        lck2 = ld(beB, W["l3f_bebc"], "l3w")
        b.wait(TE, b.ck("d_l3w"))
        b.wait(TE, b.ck("d_l3we"))
        wg3 = wgt.rearrange("p (c e) -> p c e", e=8)
        for tt in range(NT):
            tsl = slice(tt * 128, tt * 128 + 128)
            b.guard("ps6", TE)
            for c in range(NCH):
                b.mm(b.PS6[:, 0:8], b.XN[:, c, tsl], wg3[:, c, :],
                     start=(c == 0), stop=(c == NCH - 1))
            pel = b.mark(TE)
            b.wait(V, pel)
            b.tt(V, b.SCF[:, 0:8], b.PS6[:, 0:8], brange("l3f_bgrow", 8),
                 OP.add)
            b.drainall(V)
            b.red(V, b.COLS[:, 0:1], b.SCF[:, 0:8], OP.max, negate=True)
            vl = b.mark(V)
            b.setguard("ps6", vl)
            b.wait(A, vl)
            b.act(b.SCF[:, 8:16], b.SCF[:, 0:8], AF.Exp,
                  bias=b.COLS[:, 0:1], accum_out=b.COLS[:, 1:2])
            ae = b.mark(A)
            b.wait(V, ae)
            b.recip(b.COLS[:, 2:3], b.COLS[:, 1:2])
            vbp = b.mark(V)
            b.guard("ps4", TE)
            b.guard("ps5", TE)
            for c in range(NCH):
                b.mm(b.PS4[:, :], b.XN[:, c, tsl], wev[:, c, 0:512],
                     start=(c == 0), stop=(c == NCH - 1))
                b.mm(b.PS5[:, :], b.XN[:, c, tsl], wev[:, c, 512:1024],
                     start=(c == 0), stop=(c == NCH - 1))
            pee = b.mark(TE)
            b.wait(V, pee)
            b.wait(V, vbp)
            ebf = b.WS[:, 0:1024]
            b.guard("l3estg", V)
            for hf in range(2):
                ps = (b.PS4, b.PS5)[hf]
                b.tt(V, b.SCF[:, 16:528], ps[:, :],
                     beB[:, hf * 512:hf * 512 + 512], OP.add)
                b.ts(V, ebf[:, hf * 512:hf * 512 + 512], b.SCF[:, 16:528],
                     b.COLS[:, 2:3], None, OP.mult)
            veb = b.mark(V)
            b.setguard("ps4", veb)
            b.setguard("ps5", veb)
            b.wait(TE, veb)
            for oc in range(NCH):
                b.guard("pt", TE)
                b.tr(b.PT[:, 0:128], ebf[:, oc * 128:oc * 128 + 128], IDB)
                ptc = b.mark(TE)
                b.wait(V, ptc)
                b.tt(V, b.XR[:, oc, tsl], b.XR[:, oc, tsl], b.PT[:, 0:128],
                     OP.add)
                b.setguard("pt", b.mark(V))
            b.setguard("l3estg", b.ck("ve"))
        tap("l3f_out", [(b.XR[:, c2, 0:256], 256) for c2 in range(NCH)])

    def emit_l3_attn(bi, ln_ck):
        SCFbf = b.SCF[:, :].bitcast(BF16)
        phase_sync()
        for c in range(NCH):
            b.inc(G.dma_start(b.XN[:, c, :], l3x2_d[c, :, :]),
                  b.dsem("l3xl"), 16)
        xn2ck = b.ck("d_l3xl")
        for eng in (TE, V, A):
            b.wait(eng, xn2ck)
        tap("l3_xn2", [(b.XN[:, c2, 512:768], 256) for c2 in range(NCH)])
        phase_sync()

        def evk(oc, nb, ps, tsl, pe_ck):
            b.wait(V, pe_ck)
            b.ts(V, BA[:, oc, tsl], ps, bcol("l3_inprojb", NCH + oc), None,
                 OP.add)
            return b.mark(V)
        proj_wstat(W["l3_ink_swz"], b.XN[:, :, :], NCH, evk)
        vtok = b.BIGB[:, :].rearrange("p (tt e) -> p tt e", e=1024)

        def evv(oc, nb, ps, tsl, pe_ck):
            b.wait(V, pe_ck)
            stg = SCFbf[:, 0:512]
            b.guard("l2vstg", V)
            b.ts(V, stg, ps, bcol("l3_inprojb", 2 * NCH + oc), None, OP.add)
            vs = b.mark(V)
            b.wait(TE, vs)
            b.guard("pt", TE)
            for j in range(4):
                b.tr(b.PT[:, j * 128:j * 128 + 128],
                     stg[:, j * 128:j * 128 + 128], IDB)
            pck = b.mark(TE)
            b.wait(V, pck)
            for j in range(4):
                tt2 = nb * 4 + j
                b.copy(V, vtok[:, tt2, oc * 128:oc * 128 + 128],
                       b.PT[:, j * 128:j * 128 + 128])
            vc = b.mark(V)
            b.setguard("pt", vc)
            b.setguard("l2vstg", vc)
            return vc
        proj_wstat(W["l3_inv_swz"], b.XN[:, :, :], NCH, evv)
        phase_sync()
        qts = SCFbf[:, 0:4096].rearrange("p (c e) -> p c e", e=512)
        for qb in range(4):
            tslq = slice(qb * 512, qb * 512 + 512)
            slab_g = [b.ck("pe"), b.ck("pe")]
            for oc in range(NCH):
                slab = b.WS[:, (oc % 2) * 1024:(oc % 2) * 1024 + 1024]
                slab3 = slab.rearrange("p (c e) -> p c e", e=128)
                b.wait(G, slab_g[oc % 2])
                b.wait(G, b.ck("ve"))
                lck = ld(slab3, W["l3_inq_swz"][oc], "ws%d" % (oc % 2))
                b.wait(TE, lck)
                pt2 = (b.PS4, b.PS5)[oc % 2]
                pname = ("ps4", "ps5")[oc % 2]
                b.guard(pname, TE)
                for c in range(NCH):
                    b.mm(pt2[:, 0:512], slab3[:, c, :], b.XN[:, c, tslq],
                         start=(c == 0), stop=(c == NCH - 1))
                pec = b.mark(TE)
                slab_g[oc % 2] = pec
                b.wait(V, pec)
                b.ts(V, qts[:, oc, :], pt2[:, 0:512],
                     bcol("l3_inprojb", oc), None, OP.add)
                b.setguard(pname, b.mark(V))
            vq = b.mark(V)
            b.wait(TE, vq)
            for j in range(4):
                qt = qb * 4 + j
                ystg = b.WS[:, 0:1024]
                b.guard("l3ystg", TE)
                for h in range(16):
                    oc = h // 2
                    pr = (h % 2) * 64
                    b.guard("psb", TE)
                    for sg in range(4):
                        s0 = sg * 512
                        b.mm(b.PSB[:, s0:s0 + 512],
                             qts[pr:pr + 64, oc, j * 128:j * 128 + 128],
                             BA[pr:pr + 64, oc, s0:s0 + 512],
                             tile_position=(pr, 0))
                    psc = b.mark(TE)
                    b.wait(V, psc)
                    b.red(V, b.COLS[:, 0:1], b.PSB[:, 0:T], OP.max,
                          negate=True)
                    b.ts(V, b.COLS[:, 1:2], b.COLS[:, 0:1], 1.0 / 8, None,
                         OP.mult)
                    vmx = b.mark(V)
                    b.wait(A, vmx)
                    Pw = b.WS[:, 2048:4096]
                    b.act(Pw, b.PSB[:, 0:T], AF.Exp, scale=1.0 / 8,
                          bias=b.COLS[:, 1:2], accum_out=b.COLS[:, 2:3])
                    aex = b.mark(A)
                    b.setguard("psb", aex)
                    b.wait(V, aex)
                    b.recip(b.COLS[:, 3:4], b.COLS[:, 2:3])
                    vrc = b.mark(V)
                    b.wait(TE, aex)
                    b.guard("ps6", TE)
                    for kb in range(NT):
                        b.guard("pt", TE)
                        b.tr(b.PT[:, 0:128], Pw[:, kb * 128:kb * 128 + 128],
                             IDB)
                        ptc = b.mark(TE)
                        b.wait(V, ptc)
                        b.copy(V, SCFbf[:, 4096:4224], b.PT[:, 0:128])
                        vpt = b.mark(V)
                        b.setguard("pt", vpt)
                        b.wait(TE, vpt)
                        b.mm(b.PS6[:, 0:64], SCFbf[:, 4096:4224],
                             vtok[:, kb, h * 64:h * 64 + 64],
                             start=(kb == 0), stop=(kb == NT - 1))
                    pey = b.mark(TE)
                    b.wait(V, pey)
                    b.wait(V, vrc)
                    b.ts(V, ystg[:, h * 64:h * 64 + 64], b.PS6[:, 0:64],
                         b.COLS[:, 3:4], None, OP.mult)
                    vyb = b.mark(V)
                    b.setguard("ps6", vyb)
                vys = b.mark(V)
                b.wait(TE, vys)
                aoT = b.WS[:, 1024:2048].rearrange("p (c e) -> p c e", e=128)
                for oc in range(NCH):
                    b.guard("pt", TE)
                    b.tr(b.PT[:, 0:128], ystg[:, oc * 128:oc * 128 + 128],
                         IDB)
                    ptc = b.mark(TE)
                    b.wait(V, ptc)
                    b.copy(V, aoT[:, oc, :], b.PT[:, 0:128])
                    b.setguard("pt", b.mark(V))
                vao = b.mark(V)
                b.wait(TE, vao)
                slab2 = b.WS[:, 4096:4552]
                for oc in range(NCH):
                    b.guard("l3oslab", G)
                    lck = ld(slab2[:, 0:456][:, 0:128 * NCH // 4] if False
                             else b.WS[:, 4096:4552][:, 0:455] if False
                             else slab2, None, "x") if False else None
                    oslab = b.WS[:, 2048:3072].rearrange(
                        "p (c e) -> p c e", e=128)
                    b.wait(G, b.ck("act"))
                    b.wait(G, b.ck("pe"))
                    lck = ld(oslab, W["l3_outproj_swz"][oc], "l3o")
                    b.wait(TE, lck)
                    b.guard("ps6", TE)
                    for c in range(NCH):
                        b.mm(b.PS6[:, 0:128], oslab[:, c, :], aoT[:, c, :],
                             start=(c == 0), stop=(c == NCH - 1))
                    peo = b.mark(TE)
                    b.wait(V, peo)
                    b.ts(V, b.SCF[:, 2048:2176], b.PS6[:, 0:128],
                         bcol("l3_outb", oc), None, OP.add)
                    b.tt(V, b.XR[:, oc, qt * 128:qt * 128 + 128],
                         b.XR[:, oc, qt * 128:qt * 128 + 128],
                         b.SCF[:, 2048:2176], OP.add)
                    vxr = b.mark(V)
                    b.setguard("ps6", vxr)
                b.setguard("l3ystg", b.ck("ve"))
        tap("l3_attn_out", [(b.XR[:, c2, 0:256], 256) for c2 in range(NCH)])

        phase_sync()
        ba = b.BIGA[:, :]
        hS = ba[:, 0:2048].rearrange("p (c e) -> p c e", e=1024)
        gS = ba[:, 2048:4096].rearrange("p (c e) -> p c e", e=1024)
        sgS = ba[:, 4096:6144].rearrange("p (c e) -> p c e", e=1024)
        hidS = ba[:, 6144:8192].rearrange("p (c e) -> p c e", e=1024)
        dhidS = ba[:, 8192:10240].rearrange("p (c e) -> p c e", e=1024)
        dpreS = ba[:, 10240:12288].rearrange("p (c e) -> p c e", e=1024)
        dgS = ba[:, 12288:14336].rearrange("p (c e) -> p c e", e=1024)
        ctok = b.WS[:, 0:2048].rearrange("p (tt e) -> p tt e", e=256)
        xtok = b.WS[:, 2048:4096].rearrange("p (tt e) -> p tt e", e=256)
        wn = [b.SCF[:, 0:512].rearrange("p (c e) -> p c e", e=256),
              b.SCF[:, 512:1024].rearrange("p (c e) -> p c e", e=256),
              b.SCF[:, 1024:1536].rearrange("p (c e) -> p c e", e=256)]
        wnbf = [SCFbf[:, 3072:3584].rearrange("p (c e) -> p c e", e=256),
                None,
                SCFbf[:, 3584:4096].rearrange("p (c e) -> p c e", e=256)]
        HQF = BAF[:, 0:2048].rearrange("p (c e) -> p c e", e=1024)
        GQ2F = BAF[:, 2048:4096].rearrange("p (c e) -> p c e", e=1024)
        HGF = BAF[:, 4096:6144].rearrange("p (c e) -> p c e", e=1024)
        wconst = b.WS[:, 4096:4608].rearrange("p (c e) -> p c e", e=256)
        CFB = BAF[:, 7168:8192].rearrange("p (c e) -> p c e", e=512)
        wconstf = b.KVACC[:, 0:512].rearrange("p (c e) -> p c e", e=256)
        for nc2 in range(2):
            tb0 = nc2 * 1024
            for lh in range(4):
                cfm = b.XN[:, 2 * lh:2 * lh + 2, tb0:tb0 + 1024]

                def cf_block(nb3, lh=lh):
                    sl4 = slice(tb0 + nb3 * 512, tb0 + nb3 * 512 + 512)
                    b.guard("cfb", SY)
                    for c2 in range(2):
                        b.inc(SY.dma_start(CFB[:, c2, :],
                                           l3x2_d[2 * lh + c2, :, sl4]),
                              b.dsem("l3cf"), 16)
                    return b.ck("d_l3cf")

                def pfm(wap3, src3, dst_cb, nch2=2, use_cf=False):
                    for nb3 in range(2):
                        tsl3 = slice(nb3 * 512, nb3 * 512 + 512)
                        if use_cf:
                            cfk = cf_block(nb3)
                            b.wait(TE, cfk)
                        for eo in range(2):
                            pt2 = (b.PS4, b.PS5)[eo]
                            pname = ("ps4", "ps5")[eo]
                            b.guard(pname, TE)
                            for c in range(nch2):
                                if use_cf:
                                    b.mm(pt2[:, 0:512],
                                         wap3[:, c, eo * 128:eo * 128 + 128],
                                         CFB[:, c, :],
                                         start=(c == 0), stop=(c == nch2 - 1))
                                else:
                                    b.mm(pt2[:, 0:512],
                                         wap3[:, c, eo * 128:eo * 128 + 128],
                                         src3[:, c, tsl3],
                                         start=(c == 0), stop=(c == nch2 - 1))
                            pck2 = b.mark(TE)
                            cck = dst_cb(eo, nb3, pt2[:, 0:512], pck2)
                            b.setguard(pname, cck)
                        if use_cf:
                            b.setguard("cfb", b.ck("pe"))

                b.guard("l3wcf", SY)
                b.wait(SY, b.ck("pe"))
                lck = ldh(wconstf, W["l3_w0t"][lh].rearrange("c p e -> p c e"), "l3wcf")
                b.wait(TE, lck)

                def cb_gate(eo, nb3, ps, pck2):
                    b.wait(A, pck2)
                    b.wait(A, b.ck("ve"))
                    b.act(b.SCF[:, 1536:2048], ps, AF.Silu)
                    b.act(b.SCF[0:0 + 128, 1536:1536] if False else
                          hS[:, eo, slice(nb3 * 512, nb3 * 512 + 512)],
                          ps, AF.Silu) if False else None
                    ak = b.mark(A)
                    b.wait(V, ak)
                    b.copy(V, hS[:, eo, nb3 * 512:nb3 * 512 + 512],
                           b.SCF[:, 1536:2048])
                    vh = b.mark(V)
                    b.wait(A, vh)
                    b.act(b.SCF[:, 1536:2048], ps, AF.Sigmoid)
                    ak2 = b.mark(A)
                    b.wait(V, ak2)
                    b.copy(V, sgS[:, eo, nb3 * 512:nb3 * 512 + 512],
                           b.SCF[:, 1536:2048])
                    return b.mark(V)
                pfm(wconstf, None, cb_gate, use_cf=True)
                b.setguard("l3wcf", b.ck("pe"))
                if nc2 == 0 and lh == 0:
                    tap("lact_cf", [(BAF[:, 7168:8192], 1024)])
                if nc2 == 0 and lh == 0:
                    tap("lact_h", [(ba[:, 0:2048], 2048)])
                    tap("lact_sg", [(ba[:, 4096:6144], 2048)])
                b.guard("l3wcf", SY)
                b.wait(SY, b.ck("pe"))
                b.wait(SY, b.ck("act"))
                lck = ldh(wconstf, W["l3_w2t"][lh].rearrange("c p e -> p c e"), "l3wcf")
                b.wait(TE, lck)

                def cb_g2(eo, nb3, ps, pck2):
                    b.wait(V, pck2)
                    sl3 = slice(nb3 * 512, nb3 * 512 + 512)
                    b.copy(V, gS[:, eo, sl3], ps)
                    b.tt(V, hidS[:, eo, sl3], hS[:, eo, sl3], gS[:, eo, sl3],
                         OP.mult)
                    return b.mark(V)
                pfm(wconstf, None, cb_g2, use_cf=True)
                b.setguard("l3wcf", b.ck("pe"))
                if nc2 == 0 and lh == 0:
                    tap("lact_g", [(ba[:, 2048:4096], 2048)])
                    tap("lact_hid", [(ba[:, 6144:8192], 2048)])
                vcn = b.mark(V)
                b.guard("l3wcf", SY)
                b.wait(SY, b.ck("pe"))
                lck = ldh(wconstf, W["l3_w1"][lh].rearrange("c p e -> p c e"), "l3wcf")
                b.wait(TE, lck)
                b.wait(TE, vcn)

                def cb_dh(eo, nb3, ps, pck2):
                    b.wait(V, pck2)
                    sl3 = slice(nb3 * 512, nb3 * 512 + 512)
                    b.ts(V, dhidS[:, eo, sl3], ps, -1.0, None, OP.mult)
                    return b.mark(V)
                pfm(wconstf, None, cb_dh, use_cf=True)
                b.setguard("l3wcf", b.ck("pe"))
                if nc2 == 0 and lh == 0:
                    tap("lact_dhid", [(ba[:, 8192:10240], 2048)])
                for c2 in range(2):
                    b.tt(V, dgS[:, c2, :], dhidS[:, c2, :], hS[:, c2, :],
                         OP.mult)
                    b.tt(V, b.SCF[:, 0:1024], hS[:, c2, :], sgS[:, c2, :],
                         OP.mult)
                    b.tt(V, b.SCF[:, 1024:2048], hS[:, c2, :],
                         b.SCF[:, 0:1024], OP.subtract)
                    b.tt(V, b.SCF[:, 0:1024], sgS[:, c2, :],
                         b.SCF[:, 1024:2048], OP.add)
                    b.tt(V, b.SCF[:, 1024:2048], dhidS[:, c2, :],
                         gS[:, c2, :], OP.mult)
                    b.tt(V, dpreS[:, c2, :], b.SCF[:, 1024:2048],
                         b.SCF[:, 0:1024], OP.mult)
                vdp = b.mark(V)
                if nc2 == 0 and lh == 0:
                    tap("lact_dpre", [(ba[:, 10240:12288], 2048)])
                    tap("lact_dg", [(ba[:, 12288:14336], 2048)])
                def totok(src3, dst3):
                    for c2 in range(2):
                        for g4 in range(2):
                            b.guard("pt", TE)
                            for jj in range(4):
                                tti = g4 * 4 + jj
                                b.tr(b.PT[:, jj * 128:jj * 128 + 128],
                                     src3[:, c2, tti * 128:tti * 128 + 128],
                                     IDB)
                            pck2 = b.mark(TE)
                            b.wait(V, pck2)
                            for jj in range(4):
                                tti = g4 * 4 + jj
                                b.copy(V, dst3[:, tti, c2 * 128:c2 * 128 + 128],
                                       b.PT[:, jj * 128:jj * 128 + 128])
                            b.setguard("pt", b.mark(V))
                    return b.mark(V)
                b.wait(TE, vdp)
                totok(cfm, ctok)
                def dw(lhs3, rhs3, wcname, dstw, dstbf, wop=OP.subtract):
                    b.guard("l3wc", G)
                    b.wait(G, b.ck("pe"))
                    b.wait(G, b.ck("ve"))
                    lck2 = ld(wconst, W[wcname][lh].rearrange("c p e -> p c e"), "l3wc")
                    b.wait(TE, lck2)
                    for dc in range(2):
                        b.guard("ps6", TE)
                        for tti in range(8):
                            b.mm(b.PS6[:, 0:256],
                                 lhs3[:, tti, dc * 128:dc * 128 + 128],
                                 rhs3[:, tti, :],
                                 start=(tti == 0), stop=(tti == 7))
                        pck2 = b.mark(TE)
                        b.wait(V, pck2)
                        b.tt(V, dstw[:, dc, :], wconst[:, dc, :],
                             b.PS6[:, 0:256], wop)
                        if dstbf is not None:
                            b.copy(V, dstbf[:, dc, :], dstw[:, dc, :])
                        b.setguard("ps6", b.mark(V))
                    return b.mark(V)
                totok(dpreS, xtok)
                dw(ctok, xtok, "l3_w0t", wn[0], wnbf[0])
                totok(dgS, xtok)
                dw(ctok, xtok, "l3_w2t", wn[2], wnbf[2])
                totok(hidS, xtok)
                dw(xtok, ctok, "l3_w1t", wn[1], None, wop=OP.add)
                if nc2 == 0 and lh == 0:
                    tap("lact_w0n", [(b.SCF[:, 0:512], 512)])
                    tap("lact_w1n", [(b.SCF[:, 512:1024], 512)])
                    tap("lact_w2n", [(b.SCF[:, 1024:1536], 512)])
                def cb_hq(eo, nb3, ps, pck2):
                    b.wait(A, pck2)
                    b.wait(A, b.ck("ve"))
                    sl3 = slice(nb3 * 512, nb3 * 512 + 512)
                    b.act(HQF[:, eo, sl3], ps, AF.Silu)
                    return b.mark(A)
                pfm(wn[0], None, cb_hq, use_cf=True)

                def cb_gq2(eo, nb3, ps, pck2):
                    b.wait(V, pck2)
                    b.wait(V, b.ck("act"))
                    sl3 = slice(nb3 * 512, nb3 * 512 + 512)
                    b.copy(V, GQ2F[:, eo, sl3], ps)
                    b.tt(V, HGF[:, eo, sl3], HQF[:, eo, sl3],
                         GQ2F[:, eo, sl3], OP.mult)
                    return b.mark(V)
                pfm(wn[2], None, cb_gq2, use_cf=True)
                if nc2 == 0 and lh == 0:
                    pass

                def cb_oq(eo, nb3, ps, pck2):
                    b.wait(V, pck2)
                    sl3 = slice(tb0 + nb3 * 512, tb0 + nb3 * 512 + 512)
                    b.tt(V, b.XR[:, 2 * lh + eo, sl3],
                         b.XR[:, 2 * lh + eo, sl3], ps, OP.add)
                    return b.mark(V)
                for do in range(2):
                    for nb3 in range(2):
                        tsl3 = slice(nb3 * 512, nb3 * 512 + 512)
                        pt2 = (b.PS4, b.PS5)[nb3]
                        pname = ("ps4", "ps5")[nb3]
                        b.guard(pname, TE)
                        for c in range(2):
                            b.mm(pt2[:, 0:512],
                                 wn[1][:, c, do * 128:do * 128 + 128],
                                 HGF[:, c, tsl3],
                                 start=(c == 0), stop=(c == 1))
                        pck2 = b.mark(TE)
                        cck = cb_oq(do, nb3, pt2[:, 0:512], pck2)
                        b.setguard(pname, cck)
        tap("l3_out", [(b.XR[:, c2, 0:256], 256) for c2 in range(NCH)])

    if 0 in layers:
        emit_l0_ffn_wprep()

    for bi in range(nbi):
        stg = b.BIGB[:, 0:2048]
        stf = stg.bitcast(F32)
        phase_sync((SY, TE, V))
        for tt in range(NT):
            b.guard("xstg", SY)
            dck = ldh(stf, x_in[bi, tt * 128:tt * 128 + 128, :], "xin")
            b.wait(TE, dck)
            b.wait(TE, CONSTS)
            b.guard("psb", TE)
            psf = b.PSB[:, 0:1024]
            for c in range(NCH):
                b.tr(psf[:, c * 128:c * 128 + 128],
                     stf[:, c * 128:c * 128 + 128], b.IDENF[:, :])
            pex = b.mark(TE)
            b.wait(V, pex)
            for c in range(NCH):
                b.copy(V, b.XR[:, c, tt * 128:tt * 128 + 128],
                       psf[:, c * 128:c * 128 + 128])
            vck = b.mark(V)
            b.setguard("psb", vck)
            b.setguard("xstg", vck)

        for L in layers:
            lnck = emit_layernorm(precise=(L == 0))
            if L == 0:
                tap("xn", [(b.XN[:, c2, 0:256], 256) for c2 in range(NCH)])
                emit_l0_ffn(bi)
                emit_l0_attn(bi, lnck)
            elif L == 1:
                emit_l1_ffn(bi)
                emit_l1_attn(bi, lnck)
            elif L == 2:
                emit_l2_attn(bi, lnck)
                emit_l2_ffn(bi)
            elif L == 3:
                emit_l3_xn2(bi, lnck)
                emit_l3_ffn(bi)
                emit_l3_attn(bi, lnck)

        phase_sync()
        for tt in range(NT):
            b.guard("psb", TE)
            psf = b.PSB[:, 0:1024]
            for c in range(NCH):
                b.tr(psf[:, c * 128:c * 128 + 128],
                     b.XR[:, c, tt * 128:tt * 128 + 128], b.IDENF[:, :])
            peo = b.mark(TE)
            b.wait(V, peo)
            ost = b.BIGB[:, 2048:4096].bitcast(F32)
            b.guard("ost", V)
            for c in range(NCH):
                b.copy(V, ost[:, c * 128:c * 128 + 128],
                       psf[:, c * 128:c * 128 + 128])
            veo = b.mark(V)
            b.setguard("psb", veo)
            b.wait(A, veo)
            ock = b.inc(A.dma_start(out[bi, tt * 128:tt * 128 + 128, :], ost),
                        "out", 16)
            b.setguard("ost", ock)

    SY.wait_ge(b.sem_out, b.cnt["out"])
    b.close()
    return b



def _swz(wT):
    Cin, Cout = wT.shape
    return np.ascontiguousarray(
        wT.reshape(Cin // 128, 128, Cout // 128, 128).transpose(2, 1, 0, 3))


def _colmaj(v):
    v = np.asarray(v).ravel()
    n = v.shape[0]
    return np.ascontiguousarray(v.reshape(n // 128, 128).T)


def _swzp(wT, ow=64):
    Cin, Cout = wT.shape
    no = (Cout + ow - 1) // ow
    out = np.zeros((no, 128, Cin // 128, ow), np.float32)
    for o in range(no):
        blk = wT[:, o * ow:min((o + 1) * ow, Cout)]
        out[o, :, :, 0:blk.shape[1]] = blk.reshape(
            Cin // 128, 128, -1).transpose(1, 0, 2)
    return np.ascontiguousarray(out)


def _tf2x32(k0, k1, x0, x1):
    rotl = lambda x, r: ((x << np.uint32(r)) |
                         (x >> np.uint32(32 - r))).astype(np.uint32)
    ks0, ks1 = np.uint32(k0), np.uint32(k1)
    ks2 = np.uint32(ks0 ^ ks1 ^ np.uint32(0x1BD11BDA))
    R = [13, 15, 26, 6, 17, 29, 16, 24]
    x0 = (x0 + ks0).astype(np.uint32)
    x1 = (x1 + ks1).astype(np.uint32)
    keys = [(ks1, ks2), (ks2, ks0), (ks0, ks1), (ks1, ks2), (ks2, ks0)]
    for rnd in range(5):
        for r in (R[:4] if rnd % 2 == 0 else R[4:]):
            x0 = (x0 + x1).astype(np.uint32)
            x1 = rotl(x1, r) ^ x0
        ka, kb = keys[rnd]
        x0 = (x0 + ka).astype(np.uint32)
        x1 = (x1 + kb + np.uint32(rnd + 1)).astype(np.uint32)
    return x0, x1


def _tf_keep(core, nbi):
    base = np.uint64(4 * core) * np.uint64(T * C)
    i64 = base + np.arange(nbi * T * C, dtype=np.uint64)
    hi = (i64 >> np.uint64(32)).astype(np.uint32)
    lo = (i64 & np.uint64(0xFFFFFFFF)).astype(np.uint32)
    x0, x1 = _tf2x32(np.uint32(0), np.uint32(42), hi, lo)
    u = (((x0 ^ x1) >> np.uint32(9)) | np.uint32(0x3F800000)).view(
        np.float32) - np.float32(1.0)
    return (u < np.float32(0.8)).reshape(nbi, T, C)



def make_weights(inp, layers=(0, 1, 2, 3), nbi=NBI, core=0):
    w = {}
    w["iden"] = np.eye(128, dtype=np.float32)
    w["onesrow"] = np.ones((1, 128), dtype=np.float32)
    w["onescol"] = np.ones((128, 1), dtype=np.float32)
    w["ltmask"] = np.tril(np.ones((128, 128), np.float32), -1)
    w["causal_add"] = np.where(
        np.arange(128)[:, None] >= np.arange(128)[None, :], 0.0,
        -1e30).astype(np.float32)
    if 0 in layers:
        wqkv_t = np.ascontiguousarray(np.asarray(inp["L0_attn_Wqkv"]).T)
        w["l0_wq_swz"] = _swz(np.ascontiguousarray(wqkv_t[:, 0:C]))
        w["l0_wkv_t"] = np.ascontiguousarray(wqkv_t[:, C:3 * C])
        w["l0_wproj_swz"] = _swz(
            np.ascontiguousarray(np.asarray(inp["L0_attn_Wproj"]).T))
        w["l0_bproj"] = _colmaj(np.asarray(inp["L0_attn_bproj"]))
        w["l0_w1_t"] = np.ascontiguousarray(np.asarray(inp["L0_ffn_w1"]).T)
        w["l0_b1"] = _colmaj(np.asarray(inp["L0_ffn_b1"]))
        w["l0_w2_t"] = np.ascontiguousarray(np.asarray(inp["L0_ffn_w2"]).T)
        w["l0_b2"] = _colmaj(np.asarray(inp["L0_ffn_b2"]))
        km = _tf_keep(core, nbi).astype(np.uint8)
        w["l0_dropmask"] = np.ascontiguousarray(
            km.transpose(0, 2, 1).reshape(nbi, NCH, 128, T))
    if 1 in layers:
        for nm, key in (("tmx", "time_maa_x"), ("tmw", "time_maa_w"),
                        ("tmk", "time_maa_k"), ("tmv", "time_maa_v"),
                        ("tmr", "time_maa_r"), ("tmg", "time_maa_g"),
                        ("tdec", "time_decay")):
            w["l1_" + nm] = _colmaj(np.asarray(inp["L1_attn_" + key]))
        w["l1_ucol"] = _colmaj(np.asarray(inp["L1_attn_u"]))
        w["l1_lng"] = _colmaj(np.asarray(inp["L1_attn_ln_g"]))
        w["l1_lnb"] = _colmaj(np.asarray(inp["L1_attn_ln_b"]))
        w["l1_maaw1_swz"] = _swzp(np.asarray(inp["L1_attn_maa_w1"]), 64)
        w2 = np.asarray(inp["L1_attn_maa_w2"])
        w["l1_maaw2a"] = np.ascontiguousarray(
            np.concatenate([w2[0], w2[1]], axis=0))
        w["l1_maaw2b"] = np.ascontiguousarray(
            np.concatenate([w2[2], w2[3]], axis=0))
        w["l1_maaw2c"] = np.ascontiguousarray(w2[4])
        w["l1_tdw1_swz"] = _swzp(np.asarray(inp["L1_attn_td_w1"]), 64)
        w["l1_tdw2"] = np.ascontiguousarray(np.asarray(inp["L1_attn_td_w2"]))
        for nm in ("r", "k", "v", "g", "o"):
            w["l1_w%s_swz" % nm] = _swz(
                np.ascontiguousarray(np.asarray(inp["L1_attn_W" + nm]).T))
        w["l1f_w1_swz"] = _swz(np.asarray(inp["L1_ffn_w1"]))
        w["l1f_b1"] = _colmaj(np.asarray(inp["L1_ffn_b1"]))
        w["l1f_w2_swz"] = _swz(np.asarray(inp["L1_ffn_w2"]))
        w["l1f_b2"] = _colmaj(np.asarray(inp["L1_ffn_b2"]))
    if 2 in layers:
        for nm in ("q", "k", "v"):
            w["l2_w%s_swz" % nm] = _swz(
                np.ascontiguousarray(np.asarray(inp["L2_attn_W" + nm]).T))
            w["l2_b" + nm] = _colmaj(np.asarray(inp["L2_attn_b" + nm]))
        for nm in ("w1", "w2", "h1", "h2", "w3"):
            w["l2f_%s_swz" % nm] = _swz(
                np.ascontiguousarray(np.asarray(inp["L2_ffn_%s_w" % nm]).T))
        w["l2f_s1"] = _colmaj(np.asarray(inp["L2_ffn_w1_s"]))
        w["l2f_s2"] = _colmaj(np.asarray(inp["L2_ffn_w2_s"]))
        w["l2f_sh1"] = _colmaj(np.asarray(inp["L2_ffn_h1_s"]))
        w["l2f_sh2"] = _colmaj(np.asarray(inp["L2_ffn_h2_s"]))
        w["l2f_s3"] = _colmaj(np.asarray(inp["L2_ffn_w3_s"]))
    if 3 in layers:
        w["l3_lnw"] = _colmaj(np.asarray(inp["L3_attn_ln_w"]))
        w["l3_lnb"] = _colmaj(np.asarray(inp["L3_attn_ln_b"]))
        ipw = np.ascontiguousarray(np.asarray(inp["L3_attn_in_proj_w"]).T)
        w["l3_inq_swz"] = _swz(np.ascontiguousarray(ipw[:, 0:C]))
        w["l3_ink_swz"] = _swz(np.ascontiguousarray(ipw[:, C:2 * C]))
        w["l3_inv_swz"] = _swz(np.ascontiguousarray(ipw[:, 2 * C:3 * C]))
        w["l3_inprojb"] = _colmaj(np.asarray(inp["L3_attn_in_proj_b"]))
        w["l3_outproj_swz"] = _swz(
            np.ascontiguousarray(np.asarray(inp["L3_attn_out_proj_w"]).T))
        w["l3_outb"] = _colmaj(np.asarray(inp["L3_attn_out_proj_b"]))
        w0 = np.asarray(inp["L3_attn_w0"]); w1 = np.asarray(inp["L3_attn_w1"])
        w2 = np.asarray(inp["L3_attn_w2"])

        def pack256(m):
            return np.ascontiguousarray(m.reshape(2, 128, 256))
        w["l3_w0t"] = np.stack([pack256(w0[i].T) for i in range(4)])
        w["l3_w2t"] = np.stack([pack256(w2[i].T) for i in range(4)])
        w["l3_w1"] = np.stack([pack256(w1[i]) for i in range(4)])
        w["l3_w1t"] = np.stack([pack256(w1[i].T) for i in range(4)])
        w["l3f_wg"] = np.ascontiguousarray(np.asarray(inp["L3_ffn_Wg"]))
        w["l3f_bgrow"] = np.ascontiguousarray(
            np.broadcast_to(np.asarray(inp["L3_ffn_bg"])[None, :],
                            (128, 8)).copy())
        w["l3f_we"] = np.ascontiguousarray(np.asarray(inp["L3_ffn_We"]))
        w["l3f_bebc"] = np.ascontiguousarray(
            np.broadcast_to(np.asarray(inp["L3_ffn_be"])[None, :],
                            (128, C)).copy())
    return w


_CACHE = {}


def kernel(**inputs):
    if "b" not in _CACHE:
        _CACHE["b"] = build_program()
    b = _CACHE["b"]
    in_maps = []
    for core in range(NCORE):
        m = {"x": np.ascontiguousarray(
            np.asarray(inputs["L0_attn_x"])[4 * core:4 * core + 4])}
        for k, v in make_weights(inputs, core=core).items():
            m["w_" + k] = v
        in_maps.append(m)
    res = run_bass_kernel_spmd(b.nc, in_maps, core_ids=list(range(NCORE)))
    return np.concatenate([r["out"] for r in res.results], axis=0)


if __name__ == "__main__":
    bb = build_program(nbi=1, layers=(0,))
    print("built ok")
